# version 03
# baseline (speedup 1.0000x reference)
import contextlib
import numpy as np
import ml_dtypes

import concourse.bass as bass
import concourse.mybir as mybir
from concourse.bass_utils import run_bass_kernel_spmd

BF = ml_dtypes.bfloat16
F32 = mybir.dt.float32
BF16 = mybir.dt.bfloat16
AF = mybir.ActivationFunctionType
AX = mybir.AxisListType
ALU = mybir.AluOpType

B, S, DM = 2, 8192, 2048
H, DH = 16, 128
WINDOW = 1024
QCHUNK, HALO = 2048, 1024
LOCAL = QCHUNK + HALO
SLAB = 512
NSLAB = LOCAL // SLAB
NTILE = LOCAL // 128
NE, JE = 16, 1024
NPE = 18
KC = DM // 128
EPS_LN, EPS_RMS = 1e-5, 1e-6
NEG = -1e30



def _wt_chunks(w):
    wt = np.ascontiguousarray(w.T)
    return np.ascontiguousarray(
        wt.reshape(KC, 128, KC, 128).transpose(2, 1, 0, 3))


def _ug_chunks(w):
    e = w.shape[0]
    return np.ascontiguousarray(
        w.reshape(e, KC, 128, 8, 128).transpose(0, 3, 2, 1, 4))


def _d_chunks(w):
    e = w.shape[0]
    return np.ascontiguousarray(
        w.reshape(e, 8, 128, KC, 128).transpose(0, 3, 2, 1, 4))


def _host_consts(inp):
    su = np.asarray(inp["L0_ffn_Su"])[0]
    sg = np.asarray(inp["L0_ffn_Sg"])[0]
    sd = np.asarray(inp["L0_ffn_Sd"])[0]
    rw = np.ascontiguousarray(np.asarray(inp["L0_ffn_router_w"]).T)
    return {
        "wq_t": _wt_chunks(np.asarray(inp["L0_attn_Wq"])),
        "wk_t": _wt_chunks(np.asarray(inp["L0_attn_Wk"])),
        "wv_t": _wt_chunks(np.asarray(inp["L0_attn_Wv"])),
        "wo_t": _wt_chunks(np.asarray(inp["L0_attn_Wo"])),
        "wu_t": _ug_chunks(np.asarray(inp["L0_ffn_Wu"])),
        "wg_t": _ug_chunks(np.asarray(inp["L0_ffn_Wg"])),
        "wd_t": _d_chunks(np.asarray(inp["L0_ffn_Wd"])),
        "su_t": _ug_chunks(np.stack([su[:, :JE], su[:, JE:]], 0)),
        "sg_t": _ug_chunks(np.stack([sg[:, :JE], sg[:, JE:]], 0)),
        "sd_t": _d_chunks(np.stack([sd[:JE], sd[JE:]], 0)),
        "rw_t": np.ascontiguousarray(rw.reshape(KC, 128, NE).transpose(1, 0, 2)),
        "rb_t": np.ascontiguousarray(
            np.asarray(inp["L0_ffn_router_bias"]).reshape(1, NE).astype(np.float32)),
        "qnw_t": np.ascontiguousarray(
            np.asarray(inp["L0_attn_q_norm_w"]).reshape(1, 128).astype(np.float32)),
        "knw_t": np.ascontiguousarray(
            np.asarray(inp["L0_attn_k_norm_w"]).reshape(1, 128).astype(np.float32)),
    }


def _host_core_inputs(c, inp, consts):
    b = c // 4
    q0 = (c % 4) * QCHUNK
    s0 = q0 - HALO
    hs = np.asarray(inp["L0_attn_hidden_states"])[b]
    x_local = np.zeros((LOCAL, DM), dtype=np.float32)
    lo = max(0, s0)
    x_local[lo - s0:] = hs[lo:s0 + LOCAL]
    x_t = np.ascontiguousarray(
        x_local.reshape(NTILE, 128, KC, 128).transpose(0, 3, 2, 1))
    jj = np.arange(1152)[None, :]
    rr = np.arange(128)[:, None]
    allowed_base = (jj > rr) & (jj <= rr + WINDOW)
    masks = np.zeros((16, 128, 1152), dtype=np.float32)
    for i in range(16):
        a = allowed_base.copy()
        if c % 4 == 0 and i < 8:
            a = a & (jj >= WINDOW - 128 * i)
        masks[i] = np.where(a, 0.0, NEG)
    masks = masks.astype(BF)
    pos = np.maximum(s0 + np.arange(LOCAL), 0).astype(np.float64)
    inv = (10000.0 ** (-np.arange(64, dtype=np.float64) / 64.0))
    ang = pos[:, None] * inv[None, :]
    cos = np.cos(ang).T.astype(np.float32)
    sin = np.sin(ang).T.astype(np.float32)
    c2 = np.concatenate([cos, cos], 0)
    s2 = np.concatenate([sin, sin], 0)
    rope = np.stack([c2.reshape(128, NSLAB, SLAB).transpose(1, 0, 2),
                     s2.reshape(128, NSLAB, SLAB).transpose(1, 0, 2)], axis=1)
    rope = np.ascontiguousarray(rope).astype(BF)
    d = {"x_t": x_t, "mask_t": masks, "rope_t": rope}
    d.update(consts)
    return d



class Sem:
    __slots__ = ("name", "n", "h")

    def __init__(self, name):
        self.name, self.n, self.h = name, 0, None

    def inc(self, k=1):
        self.n += k
        return self.n


def _wl(*entries):
    return [e for e in entries if e is not None]



def build_program(stage=3, dbg=False, n_pe=NPE, pe_list=None, no_crep=False, no_cmul=False):
    nc = bass.Bass("TRN2", target_bir_lowering=False, debug=False)

    x_t = nc.dram_tensor("x_t", [NTILE, 128, KC, 128], F32, kind="ExternalInput").ap()
    mask_t = nc.dram_tensor("mask_t", [16, 128, 1152], BF16, kind="ExternalInput").ap()
    rope_t = nc.dram_tensor("rope_t", [NSLAB, 2, 128, SLAB], BF16, kind="ExternalInput").ap()
    wq_t = nc.dram_tensor("wq_t", [KC, 128, KC, 128], F32, kind="ExternalInput").ap()
    wk_t = nc.dram_tensor("wk_t", [KC, 128, KC, 128], F32, kind="ExternalInput").ap()
    wv_t = nc.dram_tensor("wv_t", [KC, 128, KC, 128], F32, kind="ExternalInput").ap()
    wo_t = nc.dram_tensor("wo_t", [KC, 128, KC, 128], F32, kind="ExternalInput").ap()
    wu_t = nc.dram_tensor("wu_t", [NE, 8, 128, KC, 128], F32, kind="ExternalInput").ap()
    wg_t = nc.dram_tensor("wg_t", [NE, 8, 128, KC, 128], F32, kind="ExternalInput").ap()
    wd_t = nc.dram_tensor("wd_t", [NE, KC, 128, 8, 128], F32, kind="ExternalInput").ap()
    su_t = nc.dram_tensor("su_t", [2, 8, 128, KC, 128], F32, kind="ExternalInput").ap()
    sg_t = nc.dram_tensor("sg_t", [2, 8, 128, KC, 128], F32, kind="ExternalInput").ap()
    sd_t = nc.dram_tensor("sd_t", [2, KC, 128, 8, 128], F32, kind="ExternalInput").ap()
    rw_t = nc.dram_tensor("rw_t", [128, KC, NE], F32, kind="ExternalInput").ap()
    rb_t = nc.dram_tensor("rb_t", [1, NE], F32, kind="ExternalInput").ap()
    qnw_t = nc.dram_tensor("qnw_t", [1, 128], F32, kind="ExternalInput").ap()
    knw_t = nc.dram_tensor("knw_t", [1, 128], F32, kind="ExternalInput").ap()
    out_t = nc.dram_tensor("out_t", [2, KC, 128, 1024], F32, kind="ExternalOutput").ap()
    if dbg:
        dbg_xn = nc.dram_tensor("dbg_xn", [128, 8192], F32, kind="ExternalOutput").ap()
        dbg_q = nc.dram_tensor("dbg_q", [128, 8192], F32, kind="ExternalOutput").ap()
        dbg_kc = nc.dram_tensor("dbg_kc", [128, 24576], F32, kind="ExternalOutput").ap()
        dbg_vt = nc.dram_tensor("dbg_vt", [128, 24576], F32, kind="ExternalOutput").ap()
        dbg_o = nc.dram_tensor("dbg_o", [128, 8192], F32, kind="ExternalOutput").ap()
        dbg_ptm = nc.dram_tensor("dbg_ptm", [128, 2304], F32, kind="ExternalOutput").ap()
        dbg_pts = nc.dram_tensor("dbg_pts", [128, 1152], F32, kind="ExternalOutput").ap()
        dbg_st = nc.dram_tensor("dbg_st", [128, 8], F32, kind="ExternalOutput").ap()
        dbg_msk = nc.dram_tensor("dbg_msk", [128, 1152], F32, kind="ExternalOutput").ap()
        dbg_s = nc.dram_tensor("dbg_s", [128, 1024], F32, kind="ExternalOutput").ap()
        dbg_cbt = nc.dram_tensor("dbg_cbt", [128, 1024], F32, kind="ExternalOutput").ap()
        dbg_cr = nc.dram_tensor("dbg_cr", [128, 1024], F32, kind="ExternalOutput").ap()
        dbg_he = nc.dram_tensor("dbg_he", [128, 8192], F32, kind="ExternalOutput").ap()

    es = contextlib.ExitStack()
    ops = {e: [] for e in ("tensor", "vector", "scalar", "gpsimd", "sync")}

    with es:
        ab = es.enter_context(nc.sbuf_tensor("ab", [128, 102592], BF16))
        af = es.enter_context(nc.sbuf_tensor("af", [128, 1853], F32))

        _ob = [0]

        def rbf(n, align=64):
            o = (_ob[0] + align - 1) // align * align
            _ob[0] = o + n
            assert _ob[0] <= 102592, f"bf arena overflow {_ob[0]}"
            return o

        O_KC = rbf(16 * 1536)
        O_VT = rbf(3 * 8192)
        O_XP = rbf(16 * 1024)
        O_XN = rbf(16 * 512)
        O_QO = rbf(16 * 512)
        assert O_QO == O_XN + 8192
        O_XT = rbf(2 * 2048)
        O_SQ = rbf(2048)
        O_SCR = rbf(4 * 512)
        assert O_SCR == O_XT + 6144
        O_WB = rbf(3 * 2048)
        O_MSK = rbf(1152)
        O_RP = rbf(1024)
        assert O_RP == O_MSK + 1152
        O_PTM = rbf(2 * 1152)
        O_PTS = rbf(1152)
        O_RW = rbf(KC * NE)
        O_IDB = rbf(128)
        O_ONCB = rbf(64)
        O_ONRB = rbf(128)
        O_MROT = rbf(128)
        O_HE = O_XT
        O_CR = O_MSK
        O_HS = O_MSK + 1024
        O_HG = O_MSK + 1536
        O_CBT = O_PTS
        O_OHB = O_PTM

        _of = [0]

        def rff(n, align=1):
            o = (_of[0] + align - 1) // align * align
            _of[0] = o + n
            assert _of[0] <= 1853, f"f32 arena overflow {_of[0]}"
            return o

        F_STA = rff(128)
        F_STB = rff(128)
        F_STC = rff(128)
        F_ST5 = rff(512)
        F_LG = rff(512, align=1)
        assert F_LG == F_ST5 + 512
        F_MNG = rff(2)
        F_Z = rff(2)
        F_ZR = rff(2)
        F_QNW = rff(1)
        F_KNW = rff(1)
        F_RBR = rff(16)
        F_LTM = rff(4 * 16)
        F_T2A = rff(4 * 16)
        F_T2B = rff(4 * 16)
        F_T2C = rff(4 * 16)
        F_T2D = rff(4 * 4)
        F_IDF = rff(16)
        F_ONC = rff(1)
        F_ONR = rff(128)
        F_DMY = rff(2)
        F_EPS = rff(2)
        F_STG = F_ST5
        F_TMPW = F_LG

        ps_s = es.enter_context(nc.psum_tensor("ps_s", [128, 1152], F32))
        pt_a = es.enter_context(nc.psum_tensor("pt_a", [128, 1024], BF16))
        pt_b = es.enter_context(nc.psum_tensor("pt_b", [128, 1024], BF16))
        gen_a = es.enter_context(nc.psum_tensor("gen_a", [128, 512], F32))
        gen_b = es.enter_context(nc.psum_tensor("gen_b", [128, 512], F32))
        ps_o = es.enter_context(nc.psum_tensor("ps_o", [128, 512], F32))
        GEN = [gen_a, gen_b]
        PT = [pt_a, pt_b]

        names = ["ld_x", "ld_w", "ld_m", "ld_p", "te_g", "te_s", "te_tr",
                 "te_pv", "te_sm", "te_bc", "ve_g", "ve_cov", "ve_pt",
                 "ve_o", "ae_g", "ae_cov", "ae_x", "st_o"]
        SM = {n: Sem(n) for n in names}
        for n in names:
            SM[n].h = es.enter_context(nc.semaphore(n))
        ld_x, ld_w, ld_m, ld_p = SM["ld_x"], SM["ld_w"], SM["ld_m"], SM["ld_p"]
        te_g, te_s, te_tr, te_pv = SM["te_g"], SM["te_s"], SM["te_tr"], SM["te_pv"]
        te_sm, te_bc = SM["te_sm"], SM["te_bc"]
        ve_g, ve_cov, ve_pt, ve_o = SM["ve_g"], SM["ve_cov"], SM["ve_pt"], SM["ve_o"]
        ae_g, ae_cov, ae_x, st_o = SM["ae_g"], SM["ae_cov"], SM["ae_x"], SM["st_o"]

        ident_b = nc.inline_tensor(np.eye(128, dtype=BF).view(np.uint16), "identb")
        ident_f = nc.inline_tensor(np.eye(16, dtype=np.float32), "identf")
        ones_cb = nc.inline_tensor(np.ones((128, 1), dtype=BF).view(np.uint16), "onescb")
        ones_cf = nc.inline_tensor(np.ones((128, 1), dtype=np.float32), "onescf")
        ones_rf = nc.inline_tensor(np.ones((1, 128), dtype=np.float32), "onesrf")
        ones_rb2 = nc.inline_tensor(np.ones((1, 128), dtype=BF).view(np.uint16), "onesrb")
        mrot_np = np.zeros((128, 128), dtype=BF)
        for i in range(64):
            mrot_np[64 + i, i] = -1.0
            mrot_np[i, 64 + i] = 1.0
        mrot_t = nc.inline_tensor(mrot_np.view(np.uint16), "mrot")
        eps_t = nc.inline_tensor(
            np.tile(np.array([[EPS_LN, 128.0 * EPS_RMS]], dtype=np.float32),
                    (128, 1)), "epst")
        ohb_np = np.zeros((128, NE * 128), dtype=BF)
        for e_ in range(NE):
            ohb_np[e_, e_ * 128:(e_ + 1) * 128] = 1
        ohb_t = nc.inline_tensor(ohb_np.view(np.uint16), "ohb")

        block = es.enter_context(nc.Block())

        def abf(o, n, p0=0, p=128):
            return ab[p0:p0 + p, o:o + n]

        def aff(o, n, p0=0, p=128):
            return af[p0:p0 + p, o:o + n]

        def op(eng, waits, build_fn, inc=None):
            waits = list(waits)
            if inc is not None:
                inc[0].inc(inc[1])

            def fn(e, _w=waits, _b=build_fn, _inc=inc):
                for s_, v_ in _w:
                    e.wait_ge(s_.h, v_)
                ins = _b(e)
                if _inc is not None:
                    ins.then_inc(_inc[0].h, _inc[1])
            ops[eng].append(fn)

        def wait_only(eng, waits):
            waits = list(waits)

            def fn(e, _w=waits):
                for s_, v_ in _w:
                    e.wait_ge(s_.h, v_)
            ops[eng].append(fn)

        def cover(eng, sem=None):
            sem = sem or ve_cov
            if eng == "vector":
                op(eng, [], lambda e: e.tensor_copy(aff(F_DMY, 1), aff(F_DMY + 1, 1)),
                   inc=(sem, 1))
            else:
                op(eng, [], lambda e: e.activation(aff(F_DMY, 1), aff(F_DMY + 1, 1),
                                                   AF.Identity), inc=(sem, 1))
            return (sem, sem.n)

        wslot = [0]
        wslot_free = [None] * 3
        gen_free = [None, None]
        pso_free = [None]
        pss_free = [None]
        pt_free = [None, None]
        xtile_free = [None, None]
        stg_free = [None, None]
        scr_free = [None, None]
        ptm_free = [None, None]
        sq_free = [None]

        def next_wslot():
            s = wslot[0]
            wslot[0] = (s + 1) % 3
            return s

        def load_w(src_ap, ncols, extra_waits=()):
            s = next_wslot()
            dst = abf(O_WB + s * 2048, ncols)
            if len(src_ap.shape) == 3:
                dst = dst.rearrange("p (k j) -> p k j", j=src_ap.shape[2])
            op("gpsimd", _wl(wslot_free[s], *extra_waits),
               lambda e, _d=dst, _s=src_ap: e.dma_start(out=_d, in_=_s),
               inc=(ld_w, 16))
            return s, (ld_w, ld_w.n)

        def wb(slot, cols, o=0):
            return abf(O_WB + slot * 2048 + o, cols)

        def emit_ln_tile(src_chunk_fn, sq_pair, sq_off, dst_fn, first_waits=()):
            op("scalar", _wl(*first_waits), lambda e: e.activation(
                sq_pair[0], sq_pair[1], AF.Square), inc=(ae_g, 1))
            c1 = cover("scalar", ae_cov)
            w0 = _wl(pss_free[0], c1)
            for k in range(KC):
                sK = src_chunk_fn(k)
                op("tensor", w0 if k == 0 else [],
                   lambda e, _k=k, _s=sK: e.matmul(
                       ps_s[0:1, 0:128], abf(O_ONCB, 1), _s,
                       start=(_k == 0), stop=(_k == KC - 1)))
                w0 = []
            for k in range(KC):
                op("tensor", [], lambda e, _k=k: e.matmul(
                    ps_s[0:1, 128:256], abf(O_ONCB, 1), abf(sq_off + _k * 128, 128),
                    start=(_k == 0), stop=(_k == KC - 1)),
                   inc=(te_sm, 1) if k == KC - 1 else None)
            tsum = (te_sm, te_sm.n)
            op("vector", [tsum], lambda e: e.tensor_scalar_mul(
                aff(F_STA, 128, p0=0, p=1), ps_s[0:1, 0:128], 1.0 / DM))
            op("vector", [], lambda e: e.tensor_scalar_mul(
                aff(F_STB, 128, p0=0, p=1), ps_s[0:1, 128:256], 1.0 / DM))
            op("vector", [], lambda e: e.tensor_tensor(
                aff(F_STC, 128, p0=0, p=1), aff(F_STA, 128, p0=0, p=1),
                aff(F_STA, 128, p0=0, p=1), ALU.mult))
            op("vector", [], lambda e: e.tensor_tensor(
                aff(F_STC, 128, p0=0, p=1), aff(F_STB, 128, p0=0, p=1),
                aff(F_STC, 128, p0=0, p=1), ALU.subtract), inc=(ve_g, 1))
            vv = (ve_g, ve_g.n)
            op("scalar", [vv], lambda e: e.activation(
                aff(F_STB, 128, p0=0, p=1), aff(F_STC, 128, p0=0, p=1),
                AF.Sqrt, bias=aff(F_EPS, 1, p0=0, p=1)), inc=(ae_g, 1))
            pss_free[0] = (ae_g, ae_g.n)
            sqc = cover("scalar", ae_cov)
            op("vector", [sqc], lambda e: e.reciprocal(
                aff(F_STC, 128, p0=0, p=1), aff(F_STB, 128, p0=0, p=1)))
            op("vector", [], lambda e: e.tensor_scalar_mul(
                aff(F_STA, 128, p0=0, p=1), aff(F_STA, 128, p0=0, p=1), -1.0))
            c2 = cover("vector")
            w1 = _wl(pso_free[0], c2)
            op("tensor", w1, lambda e: e.matmul(
                ps_o[:, 0:128], aff(F_ONR, 128, p0=0, p=1),
                aff(F_STA, 128, p0=0, p=1), start=True, stop=True))
            op("tensor", [], lambda e: e.matmul(
                ps_o[:, 128:256], aff(F_ONR, 128, p0=0, p=1),
                aff(F_STC, 128, p0=0, p=1), start=True, stop=True),
               inc=(te_bc, 1))
            tb = (te_bc, te_bc.n)
            for k in range(KC):
                sK = src_chunk_fn(k)
                op("vector", [tb] if k == 0 else [],
                   lambda e, _k=k, _s=sK: e.tensor_tensor(
                       abf(sq_off + _k * 128, 128), _s, ps_o[:, 0:128], ALU.add))
            for k in range(KC):
                dK = dst_fn(k)
                op("vector", [], lambda e, _k=k, _d=dK: e.tensor_tensor(
                    _d, abf(sq_off + _k * 128, 128), ps_o[:, 128:256], ALU.mult),
                   inc=(ve_g, 1) if k == KC - 1 else None)
            pso_free[0] = (ve_g, ve_g.n)
            return cover("vector")

        def rms_B(st_, first_waits=()):
            gi = st_["gi"]
            s0o, s1o = st_["s0"], st_["s1"]
            op("scalar", _wl(*first_waits),
               lambda e, _g=gi, _s0=s0o: e.activation(
                   abf(_s0, 512), GEN[_g][:], AF.Identity), inc=(ae_g, 1))
            gen_free[gi] = (ae_g, ae_g.n)
            op("scalar", [], lambda e, _s0=s0o, _s1=s1o: e.activation(
                abf(_s1, 512), abf(_s0, 512), AF.Square), inc=(ae_g, 1))
            c1 = cover("scalar", ae_cov)
            w0 = _wl(pss_free[0], c1)
            op("tensor", w0, lambda e, _s1=s1o: e.matmul(
                ps_s[0:1, 512:1024], abf(O_ONCB, 1), abf(_s1, 512),
                start=True, stop=True), inc=(te_sm, 1))
            st_["sum"] = (te_sm, te_sm.n)

        def rms_C(st_):
            op("scalar", [st_["sum"]], lambda e: e.activation(
                aff(F_ST5, 512, p0=0, p=1), ps_s[0:1, 512:1024], AF.Sqrt,
                bias=aff(F_EPS + 1, 1, p0=0, p=1)), inc=(ae_g, 1))
            pss_free[0] = (ae_g, ae_g.n)
            sqc = cover("scalar", ae_cov)
            op("vector", [sqc], lambda e: e.reciprocal(
                aff(F_ST5, 512, p0=0, p=1), aff(F_ST5, 512, p0=0, p=1)))
            post = 1.0 if st_["is_q"] else float(np.sqrt(128.0))
            op("vector", [], lambda e, _p=post: e.tensor_scalar_mul(
                aff(F_ST5, 512, p0=0, p=1), aff(F_ST5, 512, p0=0, p=1), _p))
            c2 = cover("vector")
            w1 = _wl(pso_free[0], c2)
            op("tensor", w1, lambda e: e.matmul(
                ps_o[:, 0:512], aff(F_ONR, 128, p0=0, p=1),
                aff(F_ST5, 512, p0=0, p=1), start=True, stop=True),
               inc=(te_bc, 1))
            st_["bc"] = (te_bc, te_bc.n)

        def rms_D(st_):
            s0o, s1o = st_["s0"], st_["s1"]
            dst_off = st_["dst"]
            wv_ap = aff(F_QNW if st_["is_q"] else F_KNW, 1)
            op("vector", [st_["bc"]], lambda e, _w=wv_ap, _s1=s1o, _s0=s0o:
               e.scalar_tensor_tensor(abf(_s1, 512), abf(_s0, 512), _w,
                                      ps_o[:, 0:512], ALU.mult, ALU.mult),
               inc=(ve_g, 1))
            c3 = cover("vector")
            op("tensor", [c3], lambda e, _s1=s1o: e.matmul(
                ps_o[:, 0:512], abf(O_MROT, 128), abf(_s1, 512),
                start=True, stop=True), inc=(te_bc, 1))
            rot_mm = (te_bc, te_bc.n)
            op("vector", [], lambda e, _s1=s1o, _d=dst_off: e.tensor_tensor(
                abf(_d, 512), abf(_s1, 512), abf(O_RP, 512), ALU.mult))
            op("vector", [rot_mm], lambda e: e.tensor_tensor(
                abf(O_MSK, 512), ps_o[:, 0:512], abf(O_RP + 512, 512), ALU.mult))
            op("vector", [], lambda e, _d=dst_off: e.tensor_tensor(
                abf(_d, 512), abf(_d, 512), abf(O_MSK, 512), ALU.add),
               inc=(ve_g, 1))
            pso_free[0] = (ve_g, ve_g.n)
            st_["scr_done"] = [(ve_g, ve_g.n), rot_mm]
            st_["cov"] = cover("vector")

        def emit_rms_rope(gi, dst_off, is_q, first_waits=()):
            s0o, s1o, s2o, s3o = O_SCR, O_SCR + 512, O_SCR + 1024, O_SCR + 1536
            op("scalar", _wl(*first_waits, scr_free[0], scr_free[1]),
               lambda e: e.activation(
                abf(s0o, 512), GEN[gi][:], AF.Identity), inc=(ae_g, 1))
            gen_free[gi] = (ae_g, ae_g.n)
            op("scalar", [], lambda e: e.activation(
                abf(s1o, 512), abf(s0o, 512), AF.Square), inc=(ae_g, 1))
            c1 = cover("scalar", ae_cov)
            w0 = _wl(pss_free[0], c1)
            op("tensor", w0, lambda e: e.matmul(
                ps_s[0:1, 512:1024], abf(O_ONCB, 1), abf(s1o, 512),
                start=True, stop=True), inc=(te_sm, 1))
            ts = (te_sm, te_sm.n)
            op("scalar", [ts], lambda e: e.activation(
                aff(F_ST5, 512, p0=0, p=1), ps_s[0:1, 512:1024], AF.Sqrt,
                bias=aff(F_EPS + 1, 1, p0=0, p=1)), inc=(ae_g, 1))
            pss_free[0] = (ae_g, ae_g.n)
            sqc = cover("scalar", ae_cov)
            op("vector", [sqc], lambda e: e.reciprocal(
                aff(F_ST5, 512, p0=0, p=1), aff(F_ST5, 512, p0=0, p=1)))
            post = 1.0 if is_q else float(np.sqrt(128.0))
            op("vector", [], lambda e: e.tensor_scalar_mul(
                aff(F_ST5, 512, p0=0, p=1), aff(F_ST5, 512, p0=0, p=1), post))
            c2 = cover("vector")
            w1 = _wl(pso_free[0], c2)
            op("tensor", w1, lambda e: e.matmul(
                ps_o[:, 0:512], aff(F_ONR, 128, p0=0, p=1),
                aff(F_ST5, 512, p0=0, p=1), start=True, stop=True),
               inc=(te_bc, 1))
            tb = (te_bc, te_bc.n)
            wv_ap = aff(F_QNW if is_q else F_KNW, 1)
            op("vector", [tb], lambda e, _w=wv_ap: e.scalar_tensor_tensor(
                abf(s1o, 512), abf(s0o, 512), _w, ps_o[:, 0:512],
                ALU.mult, ALU.mult), inc=(ve_g, 1))
            c3 = cover("vector")
            op("tensor", [c3], lambda e: e.matmul(
                ps_o[:, 0:512], abf(O_MROT, 128), abf(s1o, 512),
                start=True, stop=True), inc=(te_bc, 1))
            rot_mm = (te_bc, te_bc.n)
            op("vector", [], lambda e: e.tensor_tensor(
                abf(s2o, 512), abf(s1o, 512), abf(O_RP, 512), ALU.mult))
            op("vector", [rot_mm], lambda e: e.tensor_tensor(
                abf(s3o, 512), ps_o[:, 0:512], abf(O_RP + 512, 512), ALU.mult))
            op("vector", [], lambda e: e.tensor_tensor(
                abf(dst_off, 512), abf(s2o, 512), abf(s3o, 512), ALU.add),
               inc=(ve_g, 1))
            pso_free[0] = (ve_g, ve_g.n)
            scr_free[0] = (ve_g, ve_g.n)
            scr_free[1] = rot_mm
            return cover("vector")

        def emit_proj_mms(w_src, rhs_fn, gi, extra=()):
            slot, wsem = load_w(w_src, 2048)
            wl = _wl(gen_free[gi], wsem, *extra)
            for k in range(KC):
                rK = rhs_fn(k)
                op("tensor", wl if k == 0 else [],
                   lambda e, _k=k, _sl=slot, _gi=gi, _r=rK: e.matmul(
                       GEN[_gi][:, 0:_r.shape[-1]], wb(_sl, 128, _k * 128), _r,
                       start=(_k == 0), stop=(_k == KC - 1)),
                   inc=(te_g, 1) if k == KC - 1 else None)
                wl = []
            wslot_free[slot] = (te_g, te_g.n)
            return (te_g, te_g.n)

        def emit_tr_pv(info):
            par = info["par"]
            s, qb, hh = info["s"], info["qb"], info["hh"]
            base_tr = te_tr.n
            for c in range(9):
                pa = c % 2
                wl = _wl(info["pnorm"] if c == 0 else None, pt_free[pa])
                op("tensor", wl, lambda e, _c=c, _p=par, _pa=pa: e.transpose(
                    PT[_pa][:, 0:128], abf(O_PTM + _p * 1152 + _c * 128, 128),
                    abf(O_IDB, 128)), inc=(te_tr, 1))
                tt = (te_tr, te_tr.n)
                wl2 = _wl(tt, info["pv_prev"] if c == 0 else None)
                op("scalar", wl2, lambda e, _c=c, _pa=pa: e.activation(
                    abf(O_PTS + _c * 128, 128), PT[_pa][:, 0:128], AF.Identity),
                   inc=(ve_pt, 1))
                pt_free[pa] = (ve_pt, ve_pt.n)
            ptm_free[par] = (te_tr, te_tr.n)
            cpt = cover("scalar", ae_cov)
            wl = _wl(cpt, pso_free[0])
            for c in range(9):
                gt = 4 * s + qb - 8 + c
                ssl = (gt // 4) % 3
                a = gt % 4
                vsrc = abf(O_VT + ssl * 8192 + a * 2048 + hh * 128, 128)
                op("tensor", wl, lambda e, _v=vsrc, _c=c: e.matmul(
                    ps_o[:, 0:128], _v, abf(O_PTS + _c * 128, 128),
                    start=(_c == 0), stop=(_c == 8)),
                   inc=(te_pv, 1) if c == 8 else None)
                wl = []
            pv_mark = (te_pv, te_pv.n)
            info["pv_mark"] = pv_mark
            return pv_mark

        def emit_o_evac(info):
            dst = abf(O_QO + info["hh"] * 512 + info["qb"] * 128, 128)
            op("scalar", [info["pv_mark"]], lambda e, _d=dst: e.activation(
                _d, ps_o[:, 0:128], AF.Identity), inc=(ve_o, 1))
            pso_free[0] = (ve_o, ve_o.n)

        def emit_crep(pe, extra=()):
            wl = _wl(pso_free[0], *extra)
            for tcc in range(2):
                if tcc > 0:
                    wl = _wl(pso_free[0])
                op("tensor", wl, lambda e, _t=tcc, _pe=pe: e.matmul(
                    ps_o[:, 0:512], abf(O_OHB + _pe * 128, 128),
                    abf(O_CBT + _t * 512, 512),
                    start=True, stop=True), inc=(te_bc, 1))
                tb = (te_bc, te_bc.n)
                op("vector", [tb], lambda e, _t=tcc: e.tensor_copy(
                    abf(O_CR + _t * 512, 512), ps_o[:, 0:512]), inc=(ve_g, 1))
                pso_free[0] = (ve_g, ve_g.n)

        def xp_ap(kcol, t0, n):
            return abf(O_XP + kcol * 1024 + t0, n)

        def emit_moe(g, moe_state):
            op("sync", [(SM["te_tr"], SM["te_tr"].n), (ae_x, ae_x.n)],
               lambda e: e.dma_start(
                   out=abf(O_OHB, NE * 128).bitcast(mybir.dt.uint16),
                   in_=ohb_t.ap()), inc=(ld_p, 16))
            ohb_ld = (ld_p, ld_p.n)
            xnp_cov = None
            for a in range(8):
                t0 = a * 128
                xp3 = ab[:, O_XP:O_XP + 16384].rearrange(
                    "p (k t) -> p k t", t=1024)[:, :, t0:t0 + 128]
                sq3 = abf(O_SQ, 2048).rearrange("p (k t) -> p k t", t=128)
                srcf = (lambda _t0=t0: (lambda k: xp_ap(k, _t0, 128)))()
                dstf = (lambda _t0=t0: (lambda k: abf(O_XN + k * 1024 + _t0, 128)))()
                fw = _wl(moe_state["xp_ready"] if a == 0 else None, sq_free[0])
                xnp_cov = emit_ln_tile(srcf, (sq3, xp3), O_SQ, dstf,
                                       first_waits=fw)
                sq_free[0] = xnp_cov
            for tc in range(2):
                wl = _wl(pss_free[0], xnp_cov if tc == 0 else None)
                for k in range(KC):
                    op("tensor", wl if k == 0 else [],
                       lambda e, _k=k, _tc=tc: e.matmul(
                           ps_s[0:16, 0:512], abf(O_RW + _k * NE, NE),
                           abf(O_XN + _k * 1024 + _tc * 512, 512),
                           start=(_k == 0), stop=(_k == KC - 1)),
                       inc=(te_sm, 1) if k == KC - 1 else None)
                    wl = []
                tlg = (te_sm, te_sm.n)
                op("vector", [tlg], lambda e: e.tensor_copy(
                    aff(F_LG, 512, p0=0, p=16), ps_s[0:16, 0:512]), inc=(ve_g, 1))
                pss_free[0] = (ve_g, ve_g.n)
                lg_cov = cover("vector")
                for j in range(4):
                    gi = j % 2
                    wl = _wl(gen_free[gi], lg_cov if j < 2 else None)
                    op("tensor", wl, lambda e, _j=j, _gi=gi: e.transpose(
                        GEN[_gi][:, 0:16], aff(F_LG + _j * 128, 128, p0=0, p=16),
                        aff(F_IDF, 16, p0=0, p=16)), inc=(te_g, 1))
                    tt = (te_g, te_g.n)
                    op("vector", [tt], lambda e, _gi=gi, _j=j: e.tensor_copy(
                        aff(F_LTM + _j * 16, 16), GEN[_gi][:, 0:16]),
                       inc=(ve_g, 1))
                    gen_free[gi] = (ve_g, ve_g.n)
                for j in range(4):
                    op("vector", [], lambda e, _j=j: e.tensor_tensor(
                        aff(F_T2A + _j * 16, 16), aff(F_LTM + _j * 16, 16),
                        aff(F_RBR, 16), ALU.add))
                for j in range(4):
                    op("vector", [], lambda e, _j=j: e.tensor_reduce(
                        aff(F_T2D + _j * 4, 1), aff(F_T2A + _j * 16, 16),
                        AX.X, ALU.max))
                cov1 = cover("vector")
                for j in range(4):
                    op("vector", [cov1] if j == 0 else [],
                       lambda e, _j=j: e.tensor_single_scalar(
                           aff(F_T2B + _j * 16, 16), aff(F_T2A + _j * 16, 16),
                           aff(F_T2D + _j * 4, 1), ALU.is_ge))
                for j in range(4):
                    op("vector", [], lambda e, _j=j: e.scalar_tensor_tensor(
                        aff(F_T2A + _j * 16, 16), aff(F_T2B + _j * 16, 16), NEG,
                        aff(F_T2A + _j * 16, 16), ALU.mult, ALU.add))
                for j in range(4):
                    op("vector", [], lambda e, _j=j: e.tensor_reduce(
                        aff(F_T2D + _j * 4 + 1, 1), aff(F_T2A + _j * 16, 16),
                        AX.X, ALU.max))
                cov2 = cover("vector")
                for j in range(4):
                    op("vector", [cov2] if j == 0 else [],
                       lambda e, _j=j: e.tensor_single_scalar(
                           aff(F_T2A + _j * 16, 16), aff(F_T2A + _j * 16, 16),
                           aff(F_T2D + _j * 4 + 1, 1), ALU.is_ge))
                for j in range(4):
                    op("vector", [], lambda e, _j=j: e.tensor_tensor(
                        aff(F_T2B + _j * 16, 16), aff(F_T2A + _j * 16, 16),
                        aff(F_T2B + _j * 16, 16), ALU.add))
                for j in range(4):
                    op("vector", [], lambda e, _j=j: e.tensor_reduce(
                        aff(F_T2D + _j * 4 + 2, 1), aff(F_LTM + _j * 16, 16),
                        AX.X, ALU.max, negate=True))
                cov3 = cover("vector")
                for j in range(4):
                    op("scalar", [cov3] if j == 0 else [],
                       lambda e, _j=j: e.activation(
                           aff(F_T2C + _j * 16, 16), aff(F_LTM + _j * 16, 16),
                           AF.Exp, bias=aff(F_T2D + _j * 4 + 2, 1)),
                       inc=(ae_g, 1))
                cov4 = cover("scalar", ae_cov)
                for j in range(4):
                    op("vector", [cov4] if j == 0 else [],
                       lambda e, _j=j: e.tensor_tensor(
                           aff(F_T2C + _j * 16, 16), aff(F_T2C + _j * 16, 16),
                           aff(F_T2B + _j * 16, 16), ALU.mult))
                for j in range(4):
                    op("vector", [], lambda e, _j=j: e.tensor_reduce(
                        aff(F_T2D + _j * 4 + 3, 1), aff(F_T2C + _j * 16, 16),
                        AX.X, ALU.add))
                for j in range(4):
                    op("vector", [], lambda e, _j=j: e.reciprocal(
                        aff(F_T2D + _j * 4 + 3, 1), aff(F_T2D + _j * 4 + 3, 1)))
                cov5 = cover("vector")
                for j in range(4):
                    op("vector", _wl(scr_free[0], cov5 if j == 0 else None),
                       lambda e, _j=j: e.tensor_scalar_mul(
                           abf(O_SCR + _j * 64, 16), aff(F_T2C + _j * 16, 16),
                           aff(F_T2D + _j * 4 + 3, 1)), inc=(ve_g, 1))
                cmb_cov2 = cover("vector")
                for j in range(4):
                    pa = j % 2
                    wl2 = _wl(cmb_cov2 if j < 2 else None, pt_free[pa])
                    op("tensor", wl2, lambda e, _pa=pa, _j=j: e.transpose(
                        PT[_pa][0:16, 0:128], abf(O_SCR + _j * 64, 16),
                        abf(O_IDB, 128)), inc=(te_tr, 1))
                    scr_free[0] = (te_tr, te_tr.n)
                    tt2 = (te_tr, te_tr.n)
                    col = tc * 512 + j * 128
                    op("vector", [tt2], lambda e, _c=col, _pa=pa: e.tensor_copy(
                        abf(O_CBT + _c, 128, p0=0, p=16), PT[_pa][0:16, 0:128]),
                       inc=(ve_pt, 1))
                    pt_free[pa] = (ve_pt, ve_pt.n)
            cbt_cov = cover("vector")
            router_tr = (SM["te_tr"], SM["te_tr"].n)

            for pe in (range(n_pe) if pe_list is None else pe_list):
                is_sh = pe >= NE
                ei = pe - NE
                usrc = su_t[ei] if is_sh else wu_t[pe]
                gsrc = sg_t[ei] if is_sh else wg_t[pe]
                dsrc = sd_t[ei] if is_sh else wd_t[pe]
                if dbg and pe == (pe_list[0] if pe_list else 0) and g == 0:
                    op("gpsimd", [cbt_cov], lambda e: e.dma_start(
                        out=dbg_cbt[:], in_=abf(O_CBT, 1024)), inc=(st_o, 16))
                if not is_sh and not no_crep:
                    emit_crep(pe, extra=_wl(cbt_cov if pe == 0 else None,
                                            ohb_ld if pe == 0 else None))
                    if dbg and pe == (pe_list[0] if pe_list else 0) and g == 0:
                        op("gpsimd", [pso_free[0]], lambda e: e.dma_start(
                            out=dbg_cr[:], in_=abf(O_CR, 1024)), inc=(st_o, 16))
                hsfree = [None]
                for jc in range(8):
                    slu, wsu = load_w(usrc[jc], 2048)
                    slg, wsg = load_w(gsrc[jc], 2048)
                    for tcc in range(2):
                        wl = _wl(gen_free[0], wsu,
                                 xnp_cov if (pe == 0 and jc == 0 and tcc == 0) else None)
                        for k in range(KC):
                            op("tensor", wl if k == 0 else [],
                               lambda e, _k=k, _sl=slu, _t=tcc: e.matmul(
                                   GEN[0][:], wb(_sl, 128, _k * 128),
                                   abf(O_XN + _k * 1024 + _t * 512, 512),
                                   start=(_k == 0), stop=(_k == KC - 1)),
                               inc=(te_g, 1) if k == KC - 1 else None)
                            wl = []
                        mmu = (te_g, te_g.n)
                        wl = _wl(gen_free[1], wsg)
                        for k in range(KC):
                            op("tensor", wl if k == 0 else [],
                               lambda e, _k=k, _sl=slg, _t=tcc: e.matmul(
                                   GEN[1][:], wb(_sl, 128, _k * 128),
                                   abf(O_XN + _k * 1024 + _t * 512, 512),
                                   start=(_k == 0), stop=(_k == KC - 1)),
                               inc=(te_g, 1) if k == KC - 1 else None)
                            wl = []
                        mmg = (te_g, te_g.n)
                        op("scalar", _wl(mmu, hsfree[0]),
                           lambda e: e.activation(
                               abf(O_HS, 512), GEN[0][:], AF.Silu), inc=(ae_g, 1))
                        gen_free[0] = (ae_g, ae_g.n)
                        siluc = cover("scalar", ae_cov)
                        if is_sh or no_cmul:
                            op("vector", [mmg],
                               lambda e: e.tensor_copy(
                                   abf(O_HG, 512), GEN[1][:]), inc=(ve_g, 1))
                        else:
                            op("vector", [mmg],
                               lambda e, _t=tcc: e.tensor_tensor(
                                   abf(O_HG, 512), GEN[1][:],
                                   abf(O_CR + _t * 512, 512), ALU.mult),
                               inc=(ve_g, 1))
                        gen_free[1] = (ve_g, ve_g.n)
                        he_dst = O_HE + jc * 1024 + tcc * 512
                        wlh = [siluc]
                        if pe == 0 and jc == 6 and tcc == 0:
                            wlh.append(router_tr)
                        op("vector", wlh,
                           lambda e, _d=he_dst: e.tensor_tensor(
                               abf(_d, 512), abf(O_HS, 512), abf(O_HG, 512),
                               ALU.mult), inc=(ve_g, 1))
                        hsfree[0] = (ve_g, ve_g.n)
                    wslot_free[slu] = (te_g, te_g.n)
                    wslot_free[slg] = (te_g, te_g.n)
                he_cov = cover("vector")
                if dbg and pe == (pe_list[0] if pe_list else 0) and g == 0:
                    op("gpsimd", [he_cov], lambda e: e.dma_start(
                        out=dbg_he[:], in_=abf(O_HE, 8192)), inc=(st_o, 16))
                for dc in range(KC):
                    sld, wsd = load_w(dsrc[dc], 1024)
                    for tcc in range(2):
                        gi = (dc * 2 + tcc) % 2
                        wl = _wl(gen_free[gi], wsd,
                                 he_cov if (dc == 0 and tcc == 0) else None)
                        for j in range(8):
                            op("tensor", wl if j == 0 else [],
                               lambda e, _j=j, _sl=sld, _gi=gi, _t=tcc: e.matmul(
                                   GEN[_gi][:], wb(_sl, 128, _j * 128),
                                   abf(O_HE + _j * 1024 + _t * 512, 512),
                                   start=(_j == 0), stop=(_j == 7)),
                               inc=(te_g, 1) if j == 7 else None)
                            wl = []
                        mmy = (te_g, te_g.n)
                        if pe < NPE - 1:
                            op("vector", [mmy],
                               lambda e, _gi=gi, _dc=dc, _t=tcc: e.tensor_tensor(
                                   xp_ap(_dc, _t * 512, 512),
                                   xp_ap(_dc, _t * 512, 512), GEN[_gi][:], ALU.add),
                               inc=(ve_g, 1))
                            gen_free[gi] = (ve_g, ve_g.n)
                        else:
                            sg_i = (dc * 2 + tcc) % 2
                            so = F_STG + sg_i * 512
                            op("vector", _wl(mmy, stg_free[sg_i]),
                               lambda e, _gi=gi, _dc=dc, _t=tcc, _so=so:
                               e.tensor_tensor(aff(_so, 512), GEN[_gi][:],
                                               xp_ap(_dc, _t * 512, 512), ALU.add),
                               inc=(ve_g, 1))
                            gen_free[gi] = (ve_g, ve_g.n)
                            vcv = cover("vector")
                            op("sync", [vcv],
                               lambda e, _g=g, _dc=dc, _t=tcc, _so=so: e.dma_start(
                                   out=out_t[_g, _dc, :, _t * 512:(_t + 1) * 512],
                                   in_=aff(_so, 512)), inc=(st_o, 16))
                            stg_free[sg_i] = (st_o, st_o.n)
                    wslot_free[sld] = (te_g, te_g.n)
            if n_pe < NPE or pe_list is not None:
                xpc = cover("vector")
                for dc in range(KC):
                    for tcc in range(2):
                        op("gpsimd", _wl(xpc if dc == 0 and tcc == 0 else None),
                           lambda e, _g=g, _dc=dc, _t=tcc: e.dma_start(
                               out=out_t[_g, _dc, :, _t * 512:(_t + 1) * 512],
                               in_=xp_ap(_dc, _t * 512, 512)), inc=(st_o, 16))
            moe_state["xn_free"] = (te_g, te_g.n)
            moe_state["he_free"] = (te_g, te_g.n)
            moe_state["xp_free"] = (ve_g, ve_g.n) if (n_pe == NPE and pe_list is None) else (st_o, st_o.n)
            scr_free[0] = (te_g, te_g.n)
            scr_free[1] = (te_g, te_g.n)
            ptm_free[0] = (te_bc, te_bc.n)
            ptm_free[1] = (te_bc, te_bc.n)

        op("sync", [], lambda e: e.dma_start(
            out=abf(O_IDB, 128).bitcast(mybir.dt.uint16), in_=ident_b.ap()),
           inc=(ld_p, 16))
        op("sync", [], lambda e: e.dma_start(
            out=aff(F_IDF, 16, p0=0, p=16), in_=ident_f.ap()), inc=(ld_p, 16))
        op("sync", [], lambda e: e.dma_start(
            out=abf(O_ONCB, 1).bitcast(mybir.dt.uint16), in_=ones_cb.ap()),
           inc=(ld_p, 16))
        op("sync", [], lambda e: e.dma_start(
            out=aff(F_ONC, 1), in_=ones_cf.ap()), inc=(ld_p, 16))
        op("sync", [], lambda e: e.dma_start(
            out=aff(F_ONR, 128, p0=0, p=1), in_=ones_rf.ap()), inc=(ld_p, 16))
        op("sync", [], lambda e: e.dma_start(
            out=abf(O_ONRB, 128, p0=0, p=1).bitcast(mybir.dt.uint16),
            in_=ones_rb2.ap()), inc=(ld_p, 16))
        op("sync", [], lambda e: e.dma_start(
            out=abf(O_OHB, NE * 128).bitcast(mybir.dt.uint16),
            in_=ohb_t.ap()), inc=(ld_p, 16))
        ohb_ld = (ld_p, ld_p.n)
        op("sync", [], lambda e: e.dma_start(
            out=aff(F_EPS, 2), in_=eps_t.ap()), inc=(ld_p, 16))
        op("sync", [], lambda e: e.dma_start(
            out=abf(O_MROT, 128).bitcast(mybir.dt.uint16), in_=mrot_t.ap()),
           inc=(ld_p, 16))
        op("sync", [], lambda e: e.dma_start(
            out=aff(F_T2A, NE, p0=0, p=1), in_=rb_t[:]), inc=(ld_p, 16))
        op("sync", [], lambda e: e.dma_start(
            out=aff(F_TMPW, 128, p0=0, p=1), in_=qnw_t[:]), inc=(ld_p, 16))
        op("gpsimd", [], lambda e: e.dma_start(
            out=abf(O_RW, KC * NE).rearrange("p (k e) -> p k e", e=NE),
            in_=rw_t[:]), inc=(ld_w, 16))
        npre = (ld_p, ld_p.n)
        op("tensor", [npre], lambda e: e.transpose(
            ps_s[:, 0:1], aff(F_TMPW, 128, p0=0, p=1),
            aff(F_IDF, 1, p0=0, p=1)), inc=(te_g, 1))
        op("vector", [(te_g, te_g.n)], lambda e: e.tensor_copy(
            aff(F_QNW, 1), ps_s[:, 0:1]), inc=(ve_g, 1))
        op("sync", [(ve_g, ve_g.n)], lambda e: e.dma_start(
            out=aff(F_TMPW, 128, p0=0, p=1), in_=knw_t[:]), inc=(ld_p, 16))
        op("tensor", [(ld_p, ld_p.n)], lambda e: e.transpose(
            ps_s[:, 128:129], aff(F_TMPW, 128, p0=0, p=1),
            aff(F_IDF, 1, p0=0, p=1)), inc=(te_g, 1))
        op("vector", [(te_g, te_g.n)], lambda e: e.tensor_copy(
            aff(F_KNW, 1), ps_s[:, 128:129]))
        op("tensor", [], lambda e: e.matmul(
            ps_o[:, 0:16], aff(F_ONR, 128, p0=0, p=1), aff(F_T2A, NE, p0=0, p=1),
            start=True, stop=True), inc=(te_bc, 1))
        op("vector", [(te_bc, te_bc.n)], lambda e: e.tensor_copy(
            aff(F_RBR, 16), ps_o[:, 0:16]), inc=(ve_g, 1))
        pss_free[0] = (ve_g, ve_g.n)
        pso_free[0] = (ve_g, ve_g.n)

        moe_state = {"xn_free": None, "he_free": None, "xp_free": None,
                     "xp_ready": None}
        prev_slot = [None]
        mask_free = [None]
        rope_free = [None]
        wo_mm_done = [None]

        for s in range(NSLAB):
            ss = s % 3
            g = (s - 2) // 2
            spos = (s - 2) % 2

            op("sync", _wl(rope_free[0]), lambda e, _s=s: e.dma_start(
                out=abf(O_RP, 1024).rearrange("p (a t) -> p a t", t=SLAB),
                in_=rope_t[_s].rearrange("a p t -> p a t")), inc=(ld_m, 16))
            rope_ld = (ld_m, ld_m.n)

            ln_cov = None
            for a in range(4):
                t_idx = 4 * s + a
                sl2 = a % 2
                xt_off = O_XT + sl2 * 2048
                wl = _wl(xtile_free[sl2],
                         moe_state["he_free"] if a < 2 else None,
                         moe_state["xn_free"] if a < 2 else None)
                op("gpsimd", wl, lambda e, _t=t_idx, _o=xt_off: e.dma_start(
                    out=abf(_o, 2048).rearrange("p (k j) -> p k j", j=128),
                    in_=x_t[_t]), inc=(ld_x, 16))
                xl = (ld_x, ld_x.n)
                srcf = (lambda _o=xt_off: (lambda k: abf(_o + k * 128, 128)))()
                dstf = (lambda _a=a: (lambda k: abf(O_XN + k * 512 + _a * 128, 128)))()
                ln_cov = emit_ln_tile(srcf, (abf(O_SQ, 2048), abf(xt_off, 2048)),
                                      O_SQ, dstf, first_waits=[xl, sq_free[0]] if sq_free[0] else [xl])
                sq_free[0] = ln_cov
                xtile_free[sl2] = ln_cov
            xn_cov = ln_cov

            sts = {}
            scr_prev = {0: [scr_free[0]], 1: [scr_free[1]]}
            k_ready = None
            for it in range(H + 3):
                if it < H:
                    hh = it
                    gi = hh % 2
                    mm = emit_proj_mms(wk_t[hh],
                                       lambda k: abf(O_XN + k * 512, 512), gi,
                                       extra=_wl(xn_cov if hh == 0 else None))
                    sts[hh] = dict(gi=gi, is_q=False, mm=mm,
                                   s0=O_SCR + (hh % 2) * 1024,
                                   s1=O_SCR + (hh % 2) * 1024 + 512,
                                   dst=O_KC + hh * 1536 + ss * 512)
                if it - 3 >= 0:
                    rms_D(sts[it - 3])
                    scr_prev[(it - 3) % 2] = sts[it - 3]["scr_done"]
                    k_ready = sts[it - 3]["cov"]
                if it - 2 >= 0 and it - 2 < H:
                    rms_C(sts[it - 2])
                if it - 1 >= 0 and it - 1 < H:
                    hh = it - 1
                    stt_ = sts[hh]
                    rms_B(stt_, first_waits=_wl(stt_["mm"],
                                                rope_ld if hh == 0 else None,
                                                *scr_prev[hh % 2]))
                    scr_prev[hh % 2] = [None]

            scr_free[0] = sts[H - 2]["scr_done"][0]
            scr_free[1] = sts[H - 1]["scr_done"][0]

            for hh in range(H):
                gi = hh % 2
                mm = emit_proj_mms(wv_t[hh], lambda k: abf(O_XN + k * 512, 512),
                                   gi, extra=())
                vs_o = O_SCR + (hh % 2) * 1024
                op("scalar", _wl(mm, k_ready if hh == 0 else None,
                                 scr_free[hh % 2]),
                   lambda e, _o=vs_o, _gi=gi: e.activation(
                       abf(_o, 512), GEN[_gi][:], AF.Identity), inc=(ae_g, 1))
                gen_free[gi] = (ae_g, ae_g.n)
                vcov = cover("scalar", ae_cov)
                last_tr = None
                for a in range(4):
                    pa = a % 2
                    wl = _wl(vcov if a == 0 else None, pt_free[pa])
                    op("tensor", wl, lambda e, _o=vs_o, _a=a, _pa=pa: e.transpose(
                        PT[_pa][:, 0:128], abf(_o + _a * 128, 128), abf(O_IDB, 128)),
                       inc=(te_tr, 1))
                    tt = (te_tr, te_tr.n)
                    dvt = O_VT + ss * 8192 + a * 2048 + hh * 128
                    op("vector", [tt], lambda e, _d=dvt, _pa=pa: e.tensor_copy(
                        abf(_d, 128), PT[_pa][:, 0:128]), inc=(ve_pt, 1))
                    pt_free[pa] = (ve_pt, ve_pt.n)
                scr_free[hh % 2] = (te_tr, te_tr.n)
            v_ready = cover("vector")

            if dbg and s == 2:
                op("gpsimd", [xn_cov], lambda e: e.dma_start(
                    out=dbg_xn[:], in_=abf(O_XN, 8192)), inc=(st_o, 16))
                op("gpsimd", [k_ready], lambda e: e.dma_start(
                    out=dbg_kc[:], in_=abf(O_KC, 24576)), inc=(st_o, 16))
                op("gpsimd", [v_ready], lambda e: e.dma_start(
                    out=dbg_vt[:], in_=abf(O_VT, 24576)), inc=(st_o, 16))
            if s < 2:
                rope_free[0] = (ve_g, ve_g.n)
                continue

            sts = {}
            scr_prev = {0: [scr_free[0]], 1: [scr_free[1]]}
            q_cov = None
            for it in range(H + 3):
                if it < H:
                    hh = it
                    gi = hh % 2
                    mm = emit_proj_mms(wq_t[hh],
                                       lambda k: abf(O_XN + k * 512, 512), gi,
                                       extra=_wl(
                                           xn_cov if hh == 0 else None,
                                           wo_mm_done[0] if hh == 0 else None,
                                           moe_state["xn_free"] if hh == 0 else None,
                                           (ve_pt, ve_pt.n) if hh == 0 else None))
                    sts[hh] = dict(gi=gi, is_q=True, mm=mm,
                                   s0=O_SCR + (hh % 2) * 1024,
                                   s1=O_SCR + (hh % 2) * 1024 + 512,
                                   dst=O_QO + hh * 512)
                if it - 3 >= 0:
                    rms_D(sts[it - 3])
                    scr_prev[(it - 3) % 2] = sts[it - 3]["scr_done"]
                    q_cov = sts[it - 3]["cov"]
                if it - 2 >= 0 and it - 2 < H:
                    rms_C(sts[it - 2])
                if it - 1 >= 0 and it - 1 < H:
                    hh = it - 1
                    stt_ = sts[hh]
                    rms_B(stt_, first_waits=_wl(stt_["mm"], *scr_prev[hh % 2]))
                    scr_prev[hh % 2] = [None]

            scr_free[0] = sts[H - 2]["scr_done"][0]
            scr_free[1] = sts[H - 1]["scr_done"][0]
            rope_free[0] = (ve_g, ve_g.n)

            if dbg and s == 2:
                op("gpsimd", [q_cov], lambda e: e.dma_start(
                    out=dbg_q[:], in_=abf(O_QO, 8192)), inc=(st_o, 16))
            for qb in range(4):
                qbg = 4 * (s - 2) + qb
                op("sync", _wl(mask_free[0], q_cov if qb == 0 else None),
                   lambda e, _q=qbg: e.dma_start(
                       out=abf(O_MSK, 1152), in_=mask_t[_q]), inc=(ld_m, 16))
                mld = (ld_m, ld_m.n)
                bset = sorted({0, 512 - qb * 128, 1024 - qb * 128, 512, 1024,
                               1152})
                segs = []
                for c0, c1 in zip(bset[:-1], bset[1:]):
                    if c0 >= c1:
                        continue
                    if c0 < 512 - qb * 128:
                        ssl, t0 = (s - 2) % 3, qb * 128 + c0
                    elif c0 < 1024 - qb * 128:
                        ssl, t0 = (s - 1) % 3, c0 - (512 - qb * 128)
                    else:
                        ssl, t0 = s % 3, c0 - (1024 - qb * 128)
                    segs.append((c0, c1 - c0, ssl, t0))
                for hh in range(H):
                    u = qbg * H + hh
                    par = u % 2
                    wl = _wl(pss_free[0],
                             mld if hh == 0 else None,
                             q_cov if hh == 0 and qb == 0 else None,
                             k_ready if hh == 0 and qb == 0 else None,
                             v_ready if hh == 0 and qb == 0 else None)
                    qsl = abf(O_QO + hh * 512 + qb * 128, 128)
                    for (c0, n, ssl, t0) in segs:
                        ksrc = abf(O_KC + hh * 1536 + ssl * 512 + t0, n)
                        op("tensor", wl,
                           lambda e, _c=c0, _k=ksrc, _st=(c0 % 512 == 0), _q=qsl:
                           e.matmul(ps_s[:, _c:_c + _k.shape[-1]], _q, _k,
                                    start=_st, stop=False))
                        wl = []
                    for si, (c0, n, ssl, t0) in enumerate(segs):
                        op("tensor", [],
                           lambda e, _c=c0, _n=n, _l=(si == len(segs) - 1):
                           e.matmul(ps_s[:, _c:_c + _n], abf(O_IDB, 128),
                                    abf(O_MSK + _c, _n), start=False, stop=_l),
                           inc=(te_s, 1) if si == len(segs) - 1 else None)
                    if hh == H - 1:
                        mask_free[0] = (te_s, te_s.n)
                    pv_prev = None
                    pend_o = None
                    if prev_slot[0] is not None:
                        pv_prev = emit_tr_pv(prev_slot[0])
                        pend_o = prev_slot[0]
                    tsv = (te_s, te_s.n)
                    op("vector", [tsv], lambda e, _p=par: e.tensor_reduce(
                        aff(F_MNG + _p, 1), ps_s[:, 0:1152], AX.X, ALU.max,
                        negate=True))
                    vmx = cover("vector")
                    op("scalar", _wl(vmx, ptm_free[par]),
                       lambda e, _p=par: e.activation(
                           abf(O_PTM + _p * 1152, 1152), ps_s[:, 0:1152], AF.Exp,
                           bias=aff(F_MNG + _p, 1), accum_out=aff(F_Z + _p, 1)),
                       inc=(ae_x, 1))
                    pss_free[0] = (ae_x, ae_x.n)
                    aev = cover("scalar", ae_cov)
                    op("vector", [aev], lambda e, _p=par: e.reciprocal(
                        aff(F_ZR + _p, 1), aff(F_Z + _p, 1)))
                    zrc = cover("vector")
                    op("vector", [zrc], lambda e, _p=par: e.tensor_scalar_mul(
                        abf(O_PTM + _p * 1152, 1152), abf(O_PTM + _p * 1152, 1152),
                        aff(F_ZR + _p, 1)))
                    pn = cover("vector")
                    if pend_o is not None:
                        emit_o_evac(pend_o)
                    prev_slot[0] = dict(u=u, s=s, qb=qb, hh=hh, par=par,
                                        pnorm=pn, pv_prev=pv_prev)
            last_pv = emit_tr_pv(prev_slot[0])
            emit_o_evac(prev_slot[0])
            prev_slot[0] = None
            o_done = cover("scalar", ae_cov)
            if dbg and s == 2:
                op("vector", [o_done], lambda e: e.tensor_copy(
                    abf(O_SQ, 1152).bitcast(F32) if False else abf(O_SQ, 2048),
                    abf(O_SQ, 2048)))
                op("vector", [], lambda e: e.tensor_copy(
                    aff(F_LG, 512), ps_s[:, 0:512]))
                op("vector", [], lambda e: e.tensor_copy(
                    aff(F_LG + 512, 512, p0=0, p=128) if False else aff(F_ST5, 512),
                    ps_s[:, 512:1024]), inc=(ve_g, 1))
                sdump = (ve_g, ve_g.n)
                op("gpsimd", [sdump], lambda e: e.dma_start(
                    out=dbg_s[:, 0:512], in_=aff(F_LG, 512)), inc=(st_o, 16))
                op("gpsimd", [], lambda e: e.dma_start(
                    out=dbg_s[:, 512:1024], in_=aff(F_ST5, 512)), inc=(st_o, 16))
                op("gpsimd", [o_done], lambda e: e.dma_start(
                    out=dbg_o[:], in_=abf(O_QO, 8192)), inc=(st_o, 16))
                op("gpsimd", [o_done], lambda e: e.dma_start(
                    out=dbg_ptm[:], in_=abf(O_PTM, 2304)), inc=(st_o, 16))
                op("gpsimd", [o_done], lambda e: e.dma_start(
                    out=dbg_pts[:], in_=abf(O_PTS, 1152)), inc=(st_o, 16))
                op("gpsimd", [o_done], lambda e: e.dma_start(
                    out=dbg_st[:], in_=aff(F_MNG, 8).bitcast(mybir.dt.float32) if False else aff(F_MNG, 8)), inc=(st_o, 16))
                op("gpsimd", [o_done], lambda e: e.dma_start(
                    out=dbg_msk[:], in_=abf(O_MSK, 1152)), inc=(st_o, 16))

            for dc in range(KC):
                gi = dc % 2
                mm = emit_proj_mms(
                    wo_t[dc], lambda k: abf(O_QO + k * 512, 512), gi,
                    extra=_wl(o_done if dc == 0 else None,
                              moe_state["xp_free"] if dc == 0 and spos == 0 else None))
                xp_off = O_XP + dc * 1024 + spos * 512
                op("scalar", [mm], lambda e, _o=xp_off, _gi=gi: e.activation(
                    abf(_o, 512), GEN[_gi][:], AF.Identity), inc=(ae_g, 1))
                gen_free[gi] = (ae_g, ae_g.n)
            wo_mm_done[0] = (te_g, te_g.n)
            wo_evac = cover("scalar", ae_cov)
            for a in range(4):
                t_idx = 4 * s + a
                sl2 = a % 2
                xt_off = O_XT + sl2 * 2048
                wl = _wl(xtile_free[sl2], wo_evac if a == 0 else None)
                op("gpsimd", wl, lambda e, _t=t_idx, _o=xt_off: e.dma_start(
                    out=abf(_o, 2048).rearrange("p (k j) -> p k j", j=128),
                    in_=x_t[_t]), inc=(ld_x, 16))
                xl = (ld_x, ld_x.n)
                w2 = [xl] + _wl(wo_evac if a == 0 else None)
                for k in range(KC):
                    xp_off = O_XP + k * 1024 + spos * 512 + a * 128
                    op("vector", w2 if k == 0 else [],
                       lambda e, _xp=xp_off, _xo=xt_off, _k=k: e.tensor_tensor(
                           abf(_xp, 128), abf(_xp, 128), abf(_xo + _k * 128, 128),
                           ALU.add), inc=(ve_g, 1) if k == KC - 1 else None)
                    w2 = []
                xtile_free[sl2] = (ve_g, ve_g.n)
            moe_state["xp_ready"] = cover("vector")

            if spos == 1:
                if stage >= 3:
                    emit_moe(g, moe_state)
                else:
                    xpc = moe_state["xp_ready"]
                    for dc in range(KC):
                        for tcc in range(2):
                            op("gpsimd",
                               _wl(xpc if dc == 0 and tcc == 0 else None),
                               lambda e, _g=g, _dc=dc, _t=tcc: e.dma_start(
                                   out=out_t[_g, _dc, :, _t * 512:(_t + 1) * 512],
                                   in_=xp_ap(_dc, _t * 512, 512)), inc=(st_o, 16))
                    moe_state["xn_free"] = None
                    moe_state["he_free"] = None
                    moe_state["xp_free"] = (st_o, st_o.n)

        wait_only("sync", [(st_o, st_o.n)])

        @block.sync
        def _(e):
            for fn in ops["sync"]:
                fn(e)

        @block.gpsimd
        def _(e):
            for fn in ops["gpsimd"]:
                fn(e)

        @block.tensor
        def _(e):
            for fn in ops["tensor"]:
                fn(e)

        @block.vector
        def _(e):
            for fn in ops["vector"]:
                fn(e)

        @block.scalar
        def _(e):
            for fn in ops["scalar"]:
                fn(e)

    return nc



def kernel(**inputs):
    consts = _host_consts(inputs)
    in_maps = [_host_core_inputs(c, inputs, consts) for c in range(8)]
    nc = build_program(stage=3)
    res = run_bass_kernel_spmd(nc, in_maps, core_ids=list(range(8)))
    out = np.empty((B, S, DM), dtype=np.float32)
    for c in range(8):
        b = c // 4
        q0 = (c % 4) * QCHUNK
        ot = np.asarray(res.results[c]["out_t"])
        fm = ot.transpose(1, 2, 0, 3).reshape(DM, QCHUNK)
        out[b, q0:q0 + QCHUNK, :] = fm.T
    return out


# version 06 (final)
# speedup vs baseline: 1.1532x; 1.1532x over previous
import contextlib
import numpy as np
import ml_dtypes

import concourse.bass as bass
import concourse.mybir as mybir
from concourse.bass_utils import run_bass_kernel_spmd

BF = ml_dtypes.bfloat16
F32 = mybir.dt.float32
BF16 = mybir.dt.bfloat16
AF = mybir.ActivationFunctionType
AX = mybir.AxisListType
ALU = mybir.AluOpType

B, S, DM = 2, 8192, 2048
H, DH = 16, 128
WINDOW = 1024
QCHUNK, HALO = 2048, 1024
LOCAL = QCHUNK + HALO
SLAB = 512
NSLAB = LOCAL // SLAB
NTILE = LOCAL // 128
NE, JE = 16, 1024
NPE = 18
KC = DM // 128
EPS_LN, EPS_RMS = 1e-5, 1e-6
NEG = -1e30



def _wt_chunks(w):
    wt = np.ascontiguousarray(w.T)
    return np.ascontiguousarray(
        wt.reshape(KC, 128, KC, 128).transpose(2, 1, 0, 3))


def _ug_chunks(w):
    e = w.shape[0]
    return np.ascontiguousarray(
        w.reshape(e, KC, 128, 8, 128).transpose(0, 3, 2, 1, 4))


def _d_chunks(w):
    e = w.shape[0]
    return np.ascontiguousarray(
        w.reshape(e, 8, 128, KC, 128).transpose(0, 3, 2, 1, 4))


def _host_consts(inp):
    su = np.asarray(inp["L0_ffn_Su"])[0]
    sg = np.asarray(inp["L0_ffn_Sg"])[0]
    sd = np.asarray(inp["L0_ffn_Sd"])[0]
    rw = np.ascontiguousarray(np.asarray(inp["L0_ffn_router_w"]).T)
    return {
        "wq_t": _wt_chunks(np.asarray(inp["L0_attn_Wq"])),
        "wk_t": _wt_chunks(np.asarray(inp["L0_attn_Wk"])),
        "wv_t": _wt_chunks(np.asarray(inp["L0_attn_Wv"])),
        "wo_t": _wt_chunks(np.asarray(inp["L0_attn_Wo"])),
        "wu_t": _ug_chunks(np.asarray(inp["L0_ffn_Wu"])),
        "wg_t": _ug_chunks(np.asarray(inp["L0_ffn_Wg"])),
        "wd_t": _d_chunks(np.asarray(inp["L0_ffn_Wd"])),
        "su_t": _ug_chunks(np.stack([su[:, :JE], su[:, JE:]], 0)),
        "sg_t": _ug_chunks(np.stack([sg[:, :JE], sg[:, JE:]], 0)),
        "sd_t": _d_chunks(np.stack([sd[:JE], sd[JE:]], 0)),
        "rw_t": np.ascontiguousarray(rw.reshape(KC, 128, NE).transpose(1, 0, 2)),
        "rb_t": np.ascontiguousarray(
            np.asarray(inp["L0_ffn_router_bias"]).reshape(1, NE).astype(np.float32)),
        "qnw_t": np.ascontiguousarray(
            np.asarray(inp["L0_attn_q_norm_w"]).reshape(1, 128).astype(np.float32)),
        "knw_t": np.ascontiguousarray(
            np.asarray(inp["L0_attn_k_norm_w"]).reshape(1, 128).astype(np.float32)),
    }


def _host_core_inputs(c, inp, consts):
    b = c // 4
    q0 = (c % 4) * QCHUNK
    s0 = q0 - HALO
    hs = np.asarray(inp["L0_attn_hidden_states"])[b]
    x_local = np.zeros((LOCAL, DM), dtype=np.float32)
    lo = max(0, s0)
    x_local[lo - s0:] = hs[lo:s0 + LOCAL]
    x_t = np.ascontiguousarray(
        x_local.reshape(NTILE, 128, KC, 128).transpose(0, 3, 2, 1))
    jj = np.arange(1152)[None, :]
    rr = np.arange(128)[:, None]
    allowed_base = (jj > rr) & (jj <= rr + WINDOW)
    masks = np.zeros((16, 128, 1152), dtype=np.float32)
    for i in range(16):
        a = allowed_base.copy()
        if c % 4 == 0 and i < 8:
            a = a & (jj >= WINDOW - 128 * i)
        masks[i] = np.where(a, 0.0, NEG)
    masks = masks.astype(BF)
    pos = np.maximum(s0 + np.arange(LOCAL), 0).astype(np.float64)
    inv = (10000.0 ** (-np.arange(64, dtype=np.float64) / 64.0))
    ang = pos[:, None] * inv[None, :]
    cos = np.cos(ang).T.astype(np.float32)
    sin = np.sin(ang).T.astype(np.float32)
    c2 = np.concatenate([cos, cos], 0)
    s2 = np.concatenate([sin, sin], 0)
    rope = np.stack([c2.reshape(128, NSLAB, SLAB).transpose(1, 0, 2),
                     s2.reshape(128, NSLAB, SLAB).transpose(1, 0, 2)], axis=1)
    rope = np.ascontiguousarray(rope).astype(BF)
    d = {"x_t": x_t, "mask_t": masks, "rope_t": rope}
    d.update(consts)
    return d



class Sem:
    __slots__ = ("name", "n", "h")

    def __init__(self, name):
        self.name, self.n, self.h = name, 0, None

    def inc(self, k=1):
        self.n += k
        return self.n


def _wl(*entries):
    return [e for e in entries if e is not None]



def build_program(stage=3, dbg=False, n_pe=NPE, pe_list=None, no_crep=False, no_cmul=False):
    nc = bass.Bass("TRN2", target_bir_lowering=False, debug=False)

    x_t = nc.dram_tensor("x_t", [NTILE, 128, KC, 128], F32, kind="ExternalInput").ap()
    mask_t = nc.dram_tensor("mask_t", [16, 128, 1152], BF16, kind="ExternalInput").ap()
    rope_t = nc.dram_tensor("rope_t", [NSLAB, 2, 128, SLAB], BF16, kind="ExternalInput").ap()
    wq_t = nc.dram_tensor("wq_t", [KC, 128, KC, 128], F32, kind="ExternalInput").ap()
    wk_t = nc.dram_tensor("wk_t", [KC, 128, KC, 128], F32, kind="ExternalInput").ap()
    wv_t = nc.dram_tensor("wv_t", [KC, 128, KC, 128], F32, kind="ExternalInput").ap()
    wo_t = nc.dram_tensor("wo_t", [KC, 128, KC, 128], F32, kind="ExternalInput").ap()
    wu_t = nc.dram_tensor("wu_t", [NE, 8, 128, KC, 128], F32, kind="ExternalInput").ap()
    wg_t = nc.dram_tensor("wg_t", [NE, 8, 128, KC, 128], F32, kind="ExternalInput").ap()
    wd_t = nc.dram_tensor("wd_t", [NE, KC, 128, 8, 128], F32, kind="ExternalInput").ap()
    su_t = nc.dram_tensor("su_t", [2, 8, 128, KC, 128], F32, kind="ExternalInput").ap()
    sg_t = nc.dram_tensor("sg_t", [2, 8, 128, KC, 128], F32, kind="ExternalInput").ap()
    sd_t = nc.dram_tensor("sd_t", [2, KC, 128, 8, 128], F32, kind="ExternalInput").ap()
    rw_t = nc.dram_tensor("rw_t", [128, KC, NE], F32, kind="ExternalInput").ap()
    rb_t = nc.dram_tensor("rb_t", [1, NE], F32, kind="ExternalInput").ap()
    qnw_t = nc.dram_tensor("qnw_t", [1, 128], F32, kind="ExternalInput").ap()
    knw_t = nc.dram_tensor("knw_t", [1, 128], F32, kind="ExternalInput").ap()
    out_t = nc.dram_tensor("out_t", [2, KC, 128, 1024], F32, kind="ExternalOutput").ap()
    if dbg:
        dbg_xn = nc.dram_tensor("dbg_xn", [128, 8192], F32, kind="ExternalOutput").ap()
        dbg_q = nc.dram_tensor("dbg_q", [128, 8192], F32, kind="ExternalOutput").ap()
        dbg_kc = nc.dram_tensor("dbg_kc", [128, 24576], F32, kind="ExternalOutput").ap()
        dbg_vt = nc.dram_tensor("dbg_vt", [128, 24576], F32, kind="ExternalOutput").ap()
        dbg_o = nc.dram_tensor("dbg_o", [128, 8192], F32, kind="ExternalOutput").ap()
        dbg_ptm = nc.dram_tensor("dbg_ptm", [128, 2304], F32, kind="ExternalOutput").ap()
        dbg_pts = nc.dram_tensor("dbg_pts", [128, 1152], F32, kind="ExternalOutput").ap()
        dbg_st = nc.dram_tensor("dbg_st", [128, 8], F32, kind="ExternalOutput").ap()
        dbg_msk = nc.dram_tensor("dbg_msk", [128, 1152], F32, kind="ExternalOutput").ap()
        dbg_s = nc.dram_tensor("dbg_s", [128, 1024], F32, kind="ExternalOutput").ap()
        dbg_cbt = nc.dram_tensor("dbg_cbt", [128, 1024], F32, kind="ExternalOutput").ap()
        dbg_cr = nc.dram_tensor("dbg_cr", [128, 1024], F32, kind="ExternalOutput").ap()
        dbg_he = nc.dram_tensor("dbg_he", [128, 8192], F32, kind="ExternalOutput").ap()

    es = contextlib.ExitStack()
    ops = {e: [] for e in ("tensor", "vector", "scalar", "gpsimd", "sync")}

    with es:
        ab = es.enter_context(nc.sbuf_tensor("ab", [128, 102592], BF16))
        af = es.enter_context(nc.sbuf_tensor("af", [128, 1853], F32))

        _ob = [0]

        def rbf(n, align=64):
            o = (_ob[0] + align - 1) // align * align
            _ob[0] = o + n
            assert _ob[0] <= 102592, f"bf arena overflow {_ob[0]}"
            return o

        O_KC = rbf(16 * 1536)
        O_VT = rbf(3 * 8192)
        O_XP = rbf(16 * 1024)
        O_XN = rbf(16 * 512)
        O_QO = rbf(16 * 512)
        assert O_QO == O_XN + 8192
        O_XT = rbf(2 * 2048)
        O_SQ = rbf(2048)
        O_SCR = rbf(4 * 512)
        assert O_SCR == O_XT + 6144
        O_WB = rbf(3 * 2048)
        O_MSK = rbf(1152)
        O_RP = rbf(1024)
        assert O_RP == O_MSK + 1152
        O_PTM = rbf(2 * 1152)
        O_PTS = rbf(1152)
        O_RW = rbf(KC * NE)
        O_IDB = rbf(128)
        O_ONCB = rbf(64)
        O_ONRB = rbf(128)
        O_MROT = rbf(128)
        O_HE = O_XT
        O_CR = O_MSK
        O_HS = O_MSK + 1024
        O_HG = O_MSK + 1536
        O_CBT = O_PTS
        O_OHB = O_PTM

        _of = [0]

        def rff(n, align=1):
            o = (_of[0] + align - 1) // align * align
            _of[0] = o + n
            assert _of[0] <= 1853, f"f32 arena overflow {_of[0]}"
            return o

        F_STA = rff(128)
        F_STB = rff(128)
        F_STC = rff(128)
        F_ST5 = rff(512)
        F_LG = rff(512, align=1)
        assert F_LG == F_ST5 + 512
        F_MNG = rff(2)
        F_Z = rff(2)
        F_ZR = rff(2)
        F_QNW = rff(1)
        F_KNW = rff(1)
        F_RBR = rff(16)
        F_LTM = rff(4 * 16)
        F_T2A = rff(4 * 16)
        F_T2B = rff(4 * 16)
        F_T2C = rff(4 * 16)
        F_T2D = rff(4 * 4)
        F_IDF = rff(16)
        F_ONC = rff(1)
        F_ONR = rff(128)
        F_DMY = rff(2)
        F_EPS = rff(2)
        F_STG = F_ST5
        F_TMPW = F_LG

        ps_s = es.enter_context(nc.psum_tensor("ps_s", [128, 1152], F32))
        pt_a = es.enter_context(nc.psum_tensor("pt_a", [128, 1024], BF16))
        pt_b = es.enter_context(nc.psum_tensor("pt_b", [128, 1024], BF16))
        gen_a = es.enter_context(nc.psum_tensor("gen_a", [128, 512], F32))
        gen_b = es.enter_context(nc.psum_tensor("gen_b", [128, 512], F32))
        ps_o = es.enter_context(nc.psum_tensor("ps_o", [128, 512], F32))
        GEN = [gen_a, gen_b]
        PT = [pt_a, pt_b]

        names = ["ld_x", "ld_w", "ld_m", "ld_p", "te_g", "te_s", "te_tr",
                 "te_pv", "te_sm", "te_bc", "ve_g", "ve_cov", "ve_pt",
                 "ve_o", "ae_g", "ae_cov", "ae_x", "st_o"]
        SM = {n: Sem(n) for n in names}
        for n in names:
            SM[n].h = es.enter_context(nc.semaphore(n))
        ld_x, ld_w, ld_m, ld_p = SM["ld_x"], SM["ld_w"], SM["ld_m"], SM["ld_p"]
        te_g, te_s, te_tr, te_pv = SM["te_g"], SM["te_s"], SM["te_tr"], SM["te_pv"]
        te_sm, te_bc = SM["te_sm"], SM["te_bc"]
        ve_g, ve_cov, ve_pt, ve_o = SM["ve_g"], SM["ve_cov"], SM["ve_pt"], SM["ve_o"]
        ae_g, ae_cov, ae_x, st_o = SM["ae_g"], SM["ae_cov"], SM["ae_x"], SM["st_o"]

        ident_b = nc.inline_tensor(np.eye(128, dtype=BF).view(np.uint16), "identb")
        ident_f = nc.inline_tensor(np.eye(16, dtype=np.float32), "identf")
        ones_cb = nc.inline_tensor(np.ones((128, 1), dtype=BF).view(np.uint16), "onescb")
        ones_cf = nc.inline_tensor(np.ones((128, 1), dtype=np.float32), "onescf")
        ones_rf = nc.inline_tensor(np.ones((1, 128), dtype=np.float32), "onesrf")
        ones_rb2 = nc.inline_tensor(np.ones((1, 128), dtype=BF).view(np.uint16), "onesrb")
        mrot_np = np.zeros((128, 128), dtype=BF)
        for i in range(64):
            mrot_np[64 + i, i] = -1.0
            mrot_np[i, 64 + i] = 1.0
        mrot_t = nc.inline_tensor(mrot_np.view(np.uint16), "mrot")
        eps_t = nc.inline_tensor(
            np.tile(np.array([[EPS_LN, 128.0 * EPS_RMS]], dtype=np.float32),
                    (128, 1)), "epst")
        ohb_np = np.zeros((128, NE * 128), dtype=BF)
        for e_ in range(NE):
            ohb_np[e_, e_ * 128:(e_ + 1) * 128] = 1
        ohb_t = nc.inline_tensor(ohb_np.view(np.uint16), "ohb")

        block = es.enter_context(nc.Block())

        def abf(o, n, p0=0, p=128):
            return ab[p0:p0 + p, o:o + n]

        def aff(o, n, p0=0, p=128):
            return af[p0:p0 + p, o:o + n]

        def op(eng, waits, build_fn, inc=None):
            waits = list(waits)
            if inc is not None:
                inc[0].inc(inc[1])

            def fn(e, _w=waits, _b=build_fn, _inc=inc):
                for s_, v_ in _w:
                    e.wait_ge(s_.h, v_)
                ins = _b(e)
                if _inc is not None:
                    ins.then_inc(_inc[0].h, _inc[1])
            ops[eng].append(fn)

        def wait_only(eng, waits):
            waits = list(waits)

            def fn(e, _w=waits):
                for s_, v_ in _w:
                    e.wait_ge(s_.h, v_)
            ops[eng].append(fn)

        def cover(eng, sem=None):
            sem = sem or ve_cov
            if eng == "vector":
                op(eng, [], lambda e: e.tensor_copy(aff(F_DMY, 1), aff(F_DMY + 1, 1)),
                   inc=(sem, 1))
            else:
                op(eng, [], lambda e: e.activation(aff(F_DMY, 1), aff(F_DMY + 1, 1),
                                                   AF.Identity), inc=(sem, 1))
            return (sem, sem.n)

        wslot = [0]
        wslot_free = [None] * 3
        gen_free = [None, None]
        pso_free = [None]
        pss_free = [None]
        pt_free = [None, None]
        xtile_free = [None, None]
        stg_free = [None, None]
        scr_free = [None, None]
        ptm_free = [None, None]
        sq_free = [None]

        def next_wslot():
            s = wslot[0]
            wslot[0] = (s + 1) % 3
            return s

        def load_w(src_ap, ncols, extra_waits=()):
            s = next_wslot()
            dst = abf(O_WB + s * 2048, ncols)
            if len(src_ap.shape) == 3:
                dst = dst.rearrange("p (k j) -> p k j", j=src_ap.shape[2])
            op("gpsimd", _wl(wslot_free[s], *extra_waits),
               lambda e, _d=dst, _s=src_ap: e.dma_start(out=_d, in_=_s),
               inc=(ld_w, 16))
            return s, (ld_w, ld_w.n)

        def wb(slot, cols, o=0):
            return abf(O_WB + slot * 2048 + o, cols)

        def emit_ln_tile(src_chunk_fn, sq_pair, sq_off, dst_fn, first_waits=()):
            op("scalar", _wl(*first_waits), lambda e: e.activation(
                sq_pair[0], sq_pair[1], AF.Square), inc=(ae_g, 1))
            c1 = cover("scalar", ae_cov)
            w0 = _wl(pss_free[0], c1)
            for k in range(KC):
                sK = src_chunk_fn(k)
                op("tensor", w0 if k == 0 else [],
                   lambda e, _k=k, _s=sK: e.matmul(
                       ps_s[0:1, 0:128], abf(O_ONCB, 1), _s,
                       start=(_k == 0), stop=(_k == KC - 1)))
                w0 = []
            for k in range(KC):
                op("tensor", [], lambda e, _k=k: e.matmul(
                    ps_s[0:1, 128:256], abf(O_ONCB, 1), abf(sq_off + _k * 128, 128),
                    start=(_k == 0), stop=(_k == KC - 1)),
                   inc=(te_sm, 1) if k == KC - 1 else None)
            tsum = (te_sm, te_sm.n)
            op("vector", [tsum], lambda e: e.tensor_scalar_mul(
                aff(F_STA, 128, p0=0, p=1), ps_s[0:1, 0:128], 1.0 / DM))
            op("vector", [], lambda e: e.tensor_scalar_mul(
                aff(F_STB, 128, p0=0, p=1), ps_s[0:1, 128:256], 1.0 / DM))
            op("vector", [], lambda e: e.tensor_tensor(
                aff(F_STC, 128, p0=0, p=1), aff(F_STA, 128, p0=0, p=1),
                aff(F_STA, 128, p0=0, p=1), ALU.mult))
            op("vector", [], lambda e: e.tensor_tensor(
                aff(F_STC, 128, p0=0, p=1), aff(F_STB, 128, p0=0, p=1),
                aff(F_STC, 128, p0=0, p=1), ALU.subtract), inc=(ve_g, 1))
            vv = (ve_g, ve_g.n)
            op("scalar", [vv], lambda e: e.activation(
                aff(F_STB, 128, p0=0, p=1), aff(F_STC, 128, p0=0, p=1),
                AF.Sqrt, bias=aff(F_EPS, 1, p0=0, p=1)), inc=(ae_g, 1))
            pss_free[0] = (ae_g, ae_g.n)
            sqc = cover("scalar", ae_cov)
            op("vector", [sqc], lambda e: e.reciprocal(
                aff(F_STC, 128, p0=0, p=1), aff(F_STB, 128, p0=0, p=1)))
            op("vector", [], lambda e: e.tensor_scalar_mul(
                aff(F_STA, 128, p0=0, p=1), aff(F_STA, 128, p0=0, p=1), -1.0))
            c2 = cover("vector")
            w1 = _wl(pso_free[0], c2)
            op("tensor", w1, lambda e: e.matmul(
                ps_o[:, 0:128], aff(F_ONR, 128, p0=0, p=1),
                aff(F_STA, 128, p0=0, p=1), start=True, stop=True))
            op("tensor", [], lambda e: e.matmul(
                ps_o[:, 128:256], aff(F_ONR, 128, p0=0, p=1),
                aff(F_STC, 128, p0=0, p=1), start=True, stop=True),
               inc=(te_bc, 1))
            tb = (te_bc, te_bc.n)
            for k in range(KC):
                sK = src_chunk_fn(k)
                op("vector", [tb] if k == 0 else [],
                   lambda e, _k=k, _s=sK: e.tensor_tensor(
                       abf(sq_off + _k * 128, 128), _s, ps_o[:, 0:128], ALU.add))
            for k in range(KC):
                dK = dst_fn(k)
                op("vector", [], lambda e, _k=k, _d=dK: e.tensor_tensor(
                    _d, abf(sq_off + _k * 128, 128), ps_o[:, 128:256], ALU.mult),
                   inc=(ve_g, 1) if k == KC - 1 else None)
            pso_free[0] = (ve_g, ve_g.n)
            return cover("vector")

        def rms_B(st_, first_waits=()):
            gi = st_["gi"]
            s0o, s1o = st_["s0"], st_["s1"]
            op("scalar", _wl(*first_waits),
               lambda e, _g=gi, _s0=s0o: e.activation(
                   abf(_s0, 512), GEN[_g][:], AF.Identity), inc=(ae_g, 1))
            gen_free[gi] = (ae_g, ae_g.n)
            op("scalar", [], lambda e, _s0=s0o, _s1=s1o: e.activation(
                abf(_s1, 512), abf(_s0, 512), AF.Square), inc=(ae_g, 1))
            c1 = cover("scalar", ae_cov)
            w0 = _wl(pss_free[0], c1)
            op("tensor", w0, lambda e, _s1=s1o: e.matmul(
                ps_s[0:1, 512:1024], abf(O_ONCB, 1), abf(_s1, 512),
                start=True, stop=True), inc=(te_sm, 1))
            st_["sum"] = (te_sm, te_sm.n)

        def rms_C(st_):
            op("scalar", [st_["sum"]], lambda e: e.activation(
                aff(F_ST5, 512, p0=0, p=1), ps_s[0:1, 512:1024], AF.Sqrt,
                bias=aff(F_EPS + 1, 1, p0=0, p=1)), inc=(ae_g, 1))
            pss_free[0] = (ae_g, ae_g.n)
            sqc = cover("scalar", ae_cov)
            op("vector", [sqc], lambda e: e.reciprocal(
                aff(F_ST5, 512, p0=0, p=1), aff(F_ST5, 512, p0=0, p=1)))
            post = 1.0 if st_["is_q"] else float(np.sqrt(128.0))
            op("vector", [], lambda e, _p=post: e.tensor_scalar_mul(
                aff(F_ST5, 512, p0=0, p=1), aff(F_ST5, 512, p0=0, p=1), _p))
            c2 = cover("vector")
            w1 = _wl(pso_free[0], c2)
            op("tensor", w1, lambda e: e.matmul(
                ps_o[:, 0:512], aff(F_ONR, 128, p0=0, p=1),
                aff(F_ST5, 512, p0=0, p=1), start=True, stop=True),
               inc=(te_bc, 1))
            st_["bc"] = (te_bc, te_bc.n)

        def rms_D(st_):
            s0o, s1o = st_["s0"], st_["s1"]
            dst_off = st_["dst"]
            wv_ap = aff(F_QNW if st_["is_q"] else F_KNW, 1)
            op("vector", [st_["bc"]], lambda e, _w=wv_ap, _s1=s1o, _s0=s0o:
               e.scalar_tensor_tensor(abf(_s1, 512), abf(_s0, 512), _w,
                                      ps_o[:, 0:512], ALU.mult, ALU.mult),
               inc=(ve_g, 1))
            c3 = cover("vector")
            op("tensor", [c3], lambda e, _s1=s1o: e.matmul(
                ps_o[:, 0:512], abf(O_MROT, 128), abf(_s1, 512),
                start=True, stop=True), inc=(te_bc, 1))
            rot_mm = (te_bc, te_bc.n)
            op("vector", [], lambda e, _s1=s1o, _d=dst_off: e.tensor_tensor(
                abf(_d, 512), abf(_s1, 512), abf(O_RP, 512), ALU.mult))
            op("vector", [rot_mm], lambda e: e.tensor_tensor(
                abf(O_MSK, 512), ps_o[:, 0:512], abf(O_RP + 512, 512), ALU.mult))
            op("vector", [], lambda e, _d=dst_off: e.tensor_tensor(
                abf(_d, 512), abf(_d, 512), abf(O_MSK, 512), ALU.add),
               inc=(ve_g, 1))
            pso_free[0] = (ve_g, ve_g.n)
            st_["scr_done"] = [(ve_g, ve_g.n), rot_mm]
            st_["cov"] = cover("vector")

        def emit_rms_rope(gi, dst_off, is_q, first_waits=()):
            s0o, s1o, s2o, s3o = O_SCR, O_SCR + 512, O_SCR + 1024, O_SCR + 1536
            op("scalar", _wl(*first_waits, scr_free[0], scr_free[1]),
               lambda e: e.activation(
                abf(s0o, 512), GEN[gi][:], AF.Identity), inc=(ae_g, 1))
            gen_free[gi] = (ae_g, ae_g.n)
            op("scalar", [], lambda e: e.activation(
                abf(s1o, 512), abf(s0o, 512), AF.Square), inc=(ae_g, 1))
            c1 = cover("scalar", ae_cov)
            w0 = _wl(pss_free[0], c1)
            op("tensor", w0, lambda e: e.matmul(
                ps_s[0:1, 512:1024], abf(O_ONCB, 1), abf(s1o, 512),
                start=True, stop=True), inc=(te_sm, 1))
            ts = (te_sm, te_sm.n)
            op("scalar", [ts], lambda e: e.activation(
                aff(F_ST5, 512, p0=0, p=1), ps_s[0:1, 512:1024], AF.Sqrt,
                bias=aff(F_EPS + 1, 1, p0=0, p=1)), inc=(ae_g, 1))
            pss_free[0] = (ae_g, ae_g.n)
            sqc = cover("scalar", ae_cov)
            op("vector", [sqc], lambda e: e.reciprocal(
                aff(F_ST5, 512, p0=0, p=1), aff(F_ST5, 512, p0=0, p=1)))
            post = 1.0 if is_q else float(np.sqrt(128.0))
            op("vector", [], lambda e: e.tensor_scalar_mul(
                aff(F_ST5, 512, p0=0, p=1), aff(F_ST5, 512, p0=0, p=1), post))
            c2 = cover("vector")
            w1 = _wl(pso_free[0], c2)
            op("tensor", w1, lambda e: e.matmul(
                ps_o[:, 0:512], aff(F_ONR, 128, p0=0, p=1),
                aff(F_ST5, 512, p0=0, p=1), start=True, stop=True),
               inc=(te_bc, 1))
            tb = (te_bc, te_bc.n)
            wv_ap = aff(F_QNW if is_q else F_KNW, 1)
            op("vector", [tb], lambda e, _w=wv_ap: e.scalar_tensor_tensor(
                abf(s1o, 512), abf(s0o, 512), _w, ps_o[:, 0:512],
                ALU.mult, ALU.mult), inc=(ve_g, 1))
            c3 = cover("vector")
            op("tensor", [c3], lambda e: e.matmul(
                ps_o[:, 0:512], abf(O_MROT, 128), abf(s1o, 512),
                start=True, stop=True), inc=(te_bc, 1))
            rot_mm = (te_bc, te_bc.n)
            op("vector", [], lambda e: e.tensor_tensor(
                abf(s2o, 512), abf(s1o, 512), abf(O_RP, 512), ALU.mult))
            op("vector", [rot_mm], lambda e: e.tensor_tensor(
                abf(s3o, 512), ps_o[:, 0:512], abf(O_RP + 512, 512), ALU.mult))
            op("vector", [], lambda e: e.tensor_tensor(
                abf(dst_off, 512), abf(s2o, 512), abf(s3o, 512), ALU.add),
               inc=(ve_g, 1))
            pso_free[0] = (ve_g, ve_g.n)
            scr_free[0] = (ve_g, ve_g.n)
            scr_free[1] = rot_mm
            return cover("vector")

        def emit_proj_mms(w_src, rhs_fn, gi, extra=()):
            slot, wsem = load_w(w_src, 2048)
            wl = _wl(gen_free[gi], wsem, *extra)
            for k in range(KC):
                rK = rhs_fn(k)
                op("tensor", wl if k == 0 else [],
                   lambda e, _k=k, _sl=slot, _gi=gi, _r=rK: e.matmul(
                       GEN[_gi][:, 0:_r.shape[-1]], wb(_sl, 128, _k * 128), _r,
                       start=(_k == 0), stop=(_k == KC - 1)),
                   inc=(te_g, 1) if k == KC - 1 else None)
                wl = []
            wslot_free[slot] = (te_g, te_g.n)
            return (te_g, te_g.n)

        def emit_tr_pv(info):
            par = info["par"]
            s, qb, hh = info["s"], info["qb"], info["hh"]
            base_tr = te_tr.n
            evac_mark = [None] * 9
            def pv_mm(c, extra):
                gt = 4 * s + qb - 8 + c
                ssl = (gt // 4) % 3
                a = gt % 4
                vsrc = abf(O_VT + ssl * 8192 + a * 2048 + hh * 128, 128)
                op("tensor", extra, lambda e, _v=vsrc, _c=c: e.matmul(
                    ps_o[:, 0:128], _v, abf(O_PTS + _c * 128, 128),
                    start=(_c == 0), stop=(_c == 8), skip_group_check=True),
                   inc=(te_pv, 1) if c == 8 else None)
            for c in range(9):
                pa = c % 2
                wl = _wl(info["pnorm"] if c == 0 else None, pt_free[pa])
                op("tensor", wl, lambda e, _c=c, _p=par, _pa=pa: e.transpose(
                    PT[_pa][:, 0:128], abf(O_PTM + _p * 1152 + _c * 128, 128),
                    abf(O_IDB, 128)), inc=(te_tr, 1))
                tt = (te_tr, te_tr.n)
                wl2 = _wl(tt, info["pv_prev"] if c == 0 else None)
                op("scalar", wl2, lambda e, _c=c, _pa=pa: e.activation(
                    abf(O_PTS + _c * 128, 128), PT[_pa][:, 0:128], AF.Identity),
                   inc=(ve_pt, 1))
                evac_mark[c] = (ve_pt, ve_pt.n)
                pt_free[pa] = (ve_pt, ve_pt.n)
                if c >= 3:
                    pv_mm(c - 3, _wl(evac_mark[c - 2],
                                     pso_free[0] if c == 3 else None))
            ptm_free[par] = (te_tr, te_tr.n)
            cpt = cover("scalar", ae_cov)
            for c in range(6, 9):
                pv_mm(c, _wl(cpt if c == 6 else None))
            pv_mark = (te_pv, te_pv.n)
            info["pv_mark"] = pv_mark
            return pv_mark

        def emit_o_evac(info):
            dst = abf(O_QO + info["hh"] * 512 + info["qb"] * 128, 128)
            op("scalar", [info["pv_mark"]], lambda e, _d=dst: e.activation(
                _d, ps_o[:, 0:128], AF.Identity), inc=(ve_o, 1))
            pso_free[0] = (ve_o, ve_o.n)

        def emit_crep(pe, extra=()):
            wl = _wl(pso_free[0], *extra)
            for tcc in range(2):
                if tcc > 0:
                    wl = _wl(pso_free[0])
                op("tensor", wl, lambda e, _t=tcc, _pe=pe: e.matmul(
                    ps_o[:, 0:512], abf(O_OHB + _pe * 128, 128),
                    abf(O_CBT + _t * 512, 512),
                    start=True, stop=True), inc=(te_bc, 1))
                tb = (te_bc, te_bc.n)
                op("vector", [tb], lambda e, _t=tcc: e.tensor_copy(
                    abf(O_CR + _t * 512, 512), ps_o[:, 0:512]), inc=(ve_g, 1))
                pso_free[0] = (ve_g, ve_g.n)

        def xp_ap(kcol, t0, n):
            return abf(O_XP + kcol * 1024 + t0, n)

        def emit_moe(g, moe_state):
            op("sync", [(SM["te_tr"], SM["te_tr"].n), (ae_x, ae_x.n)],
               lambda e: e.dma_start(
                   out=abf(O_OHB, NE * 128).bitcast(mybir.dt.uint16),
                   in_=ohb_t.ap()), inc=(ld_p, 16))
            ohb_ld = (ld_p, ld_p.n)
            xnp_cov = None
            for a in range(8):
                t0 = a * 128
                xp3 = ab[:, O_XP:O_XP + 16384].rearrange(
                    "p (k t) -> p k t", t=1024)[:, :, t0:t0 + 128]
                sq3 = abf(O_SQ, 2048).rearrange("p (k t) -> p k t", t=128)
                srcf = (lambda _t0=t0: (lambda k: xp_ap(k, _t0, 128)))()
                dstf = (lambda _t0=t0: (lambda k: abf(O_XN + k * 1024 + _t0, 128)))()
                fw = _wl(moe_state["xp_ready"] if a == 0 else None, sq_free[0])
                xnp_cov = emit_ln_tile(srcf, (sq3, xp3), O_SQ, dstf,
                                       first_waits=fw)
                sq_free[0] = xnp_cov
            for tc in range(2):
                wl = _wl(pss_free[0], xnp_cov if tc == 0 else None)
                for k in range(KC):
                    op("tensor", wl if k == 0 else [],
                       lambda e, _k=k, _tc=tc: e.matmul(
                           ps_s[0:16, 0:512], abf(O_RW + _k * NE, NE),
                           abf(O_XN + _k * 1024 + _tc * 512, 512),
                           start=(_k == 0), stop=(_k == KC - 1)),
                       inc=(te_sm, 1) if k == KC - 1 else None)
                    wl = []
                tlg = (te_sm, te_sm.n)
                op("vector", [tlg], lambda e: e.tensor_copy(
                    aff(F_LG, 512, p0=0, p=16), ps_s[0:16, 0:512]), inc=(ve_g, 1))
                pss_free[0] = (ve_g, ve_g.n)
                lg_cov = cover("vector")
                for j in range(4):
                    gi = j % 2
                    wl = _wl(gen_free[gi], lg_cov if j < 2 else None)
                    op("tensor", wl, lambda e, _j=j, _gi=gi: e.transpose(
                        GEN[_gi][:, 0:16], aff(F_LG + _j * 128, 128, p0=0, p=16),
                        aff(F_IDF, 16, p0=0, p=16)), inc=(te_g, 1))
                    tt = (te_g, te_g.n)
                    op("vector", [tt], lambda e, _gi=gi, _j=j: e.tensor_copy(
                        aff(F_LTM + _j * 16, 16), GEN[_gi][:, 0:16]),
                       inc=(ve_g, 1))
                    gen_free[gi] = (ve_g, ve_g.n)
                for j in range(4):
                    op("vector", [], lambda e, _j=j: e.tensor_tensor(
                        aff(F_T2A + _j * 16, 16), aff(F_LTM + _j * 16, 16),
                        aff(F_RBR, 16), ALU.add))
                for j in range(4):
                    op("vector", [], lambda e, _j=j: e.tensor_reduce(
                        aff(F_T2D + _j * 4, 1), aff(F_T2A + _j * 16, 16),
                        AX.X, ALU.max))
                cov1 = cover("vector")
                for j in range(4):
                    op("vector", [cov1] if j == 0 else [],
                       lambda e, _j=j: e.tensor_single_scalar(
                           aff(F_T2B + _j * 16, 16), aff(F_T2A + _j * 16, 16),
                           aff(F_T2D + _j * 4, 1), ALU.is_ge))
                for j in range(4):
                    op("vector", [], lambda e, _j=j: e.scalar_tensor_tensor(
                        aff(F_T2A + _j * 16, 16), aff(F_T2B + _j * 16, 16), NEG,
                        aff(F_T2A + _j * 16, 16), ALU.mult, ALU.add))
                for j in range(4):
                    op("vector", [], lambda e, _j=j: e.tensor_reduce(
                        aff(F_T2D + _j * 4 + 1, 1), aff(F_T2A + _j * 16, 16),
                        AX.X, ALU.max))
                cov2 = cover("vector")
                for j in range(4):
                    op("vector", [cov2] if j == 0 else [],
                       lambda e, _j=j: e.tensor_single_scalar(
                           aff(F_T2A + _j * 16, 16), aff(F_T2A + _j * 16, 16),
                           aff(F_T2D + _j * 4 + 1, 1), ALU.is_ge))
                for j in range(4):
                    op("vector", [], lambda e, _j=j: e.tensor_tensor(
                        aff(F_T2B + _j * 16, 16), aff(F_T2A + _j * 16, 16),
                        aff(F_T2B + _j * 16, 16), ALU.add))
                for j in range(4):
                    op("vector", [], lambda e, _j=j: e.tensor_reduce(
                        aff(F_T2D + _j * 4 + 2, 1), aff(F_LTM + _j * 16, 16),
                        AX.X, ALU.max, negate=True))
                cov3 = cover("vector")
                for j in range(4):
                    op("scalar", [cov3] if j == 0 else [],
                       lambda e, _j=j: e.activation(
                           aff(F_T2C + _j * 16, 16), aff(F_LTM + _j * 16, 16),
                           AF.Exp, bias=aff(F_T2D + _j * 4 + 2, 1)),
                       inc=(ae_g, 1))
                cov4 = cover("scalar", ae_cov)
                for j in range(4):
                    op("vector", [cov4] if j == 0 else [],
                       lambda e, _j=j: e.tensor_tensor(
                           aff(F_T2C + _j * 16, 16), aff(F_T2C + _j * 16, 16),
                           aff(F_T2B + _j * 16, 16), ALU.mult))
                for j in range(4):
                    op("vector", [], lambda e, _j=j: e.tensor_reduce(
                        aff(F_T2D + _j * 4 + 3, 1), aff(F_T2C + _j * 16, 16),
                        AX.X, ALU.add))
                for j in range(4):
                    op("vector", [], lambda e, _j=j: e.reciprocal(
                        aff(F_T2D + _j * 4 + 3, 1), aff(F_T2D + _j * 4 + 3, 1)))
                cov5 = cover("vector")
                for j in range(4):
                    op("vector", _wl(scr_free[0], cov5 if j == 0 else None),
                       lambda e, _j=j: e.tensor_scalar_mul(
                           abf(O_SCR + _j * 64, 16), aff(F_T2C + _j * 16, 16),
                           aff(F_T2D + _j * 4 + 3, 1)), inc=(ve_g, 1))
                cmb_cov2 = cover("vector")
                for j in range(4):
                    pa = j % 2
                    wl2 = _wl(cmb_cov2 if j < 2 else None, pt_free[pa])
                    op("tensor", wl2, lambda e, _pa=pa, _j=j: e.transpose(
                        PT[_pa][0:16, 0:128], abf(O_SCR + _j * 64, 16),
                        abf(O_IDB, 128)), inc=(te_tr, 1))
                    scr_free[0] = (te_tr, te_tr.n)
                    tt2 = (te_tr, te_tr.n)
                    col = tc * 512 + j * 128
                    op("vector", [tt2], lambda e, _c=col, _pa=pa: e.tensor_copy(
                        abf(O_CBT + _c, 128, p0=0, p=16), PT[_pa][0:16, 0:128]),
                       inc=(ve_pt, 1))
                    pt_free[pa] = (ve_pt, ve_pt.n)
            cbt_cov = cover("vector")
            router_tr = (SM["te_tr"], SM["te_tr"].n)

            for pe in (range(n_pe) if pe_list is None else pe_list):
                is_sh = pe >= NE
                ei = pe - NE
                usrc = su_t[ei] if is_sh else wu_t[pe]
                gsrc = sg_t[ei] if is_sh else wg_t[pe]
                dsrc = sd_t[ei] if is_sh else wd_t[pe]
                if dbg and pe == (pe_list[0] if pe_list else 0) and g == 0:
                    op("gpsimd", [cbt_cov], lambda e: e.dma_start(
                        out=dbg_cbt[:], in_=abf(O_CBT, 1024)), inc=(st_o, 16))
                if not is_sh and not no_crep:
                    emit_crep(pe, extra=_wl(cbt_cov if pe == 0 else None,
                                            ohb_ld if pe == 0 else None))
                    if dbg and pe == (pe_list[0] if pe_list else 0) and g == 0:
                        op("gpsimd", [pso_free[0]], lambda e: e.dma_start(
                            out=dbg_cr[:], in_=abf(O_CR, 1024)), inc=(st_o, 16))
                hsfree = [None]
                for jc in range(8):
                    slu, wsu = load_w(usrc[jc], 2048)
                    slg, wsg = load_w(gsrc[jc], 2048)
                    for tcc in range(2):
                        wl = _wl(gen_free[0], wsu,
                                 xnp_cov if (pe == 0 and jc == 0 and tcc == 0) else None)
                        for k in range(KC):
                            op("tensor", wl if k == 0 else [],
                               lambda e, _k=k, _sl=slu, _t=tcc: e.matmul(
                                   GEN[0][:], wb(_sl, 128, _k * 128),
                                   abf(O_XN + _k * 1024 + _t * 512, 512),
                                   start=(_k == 0), stop=(_k == KC - 1)),
                               inc=(te_g, 1) if k == KC - 1 else None)
                            wl = []
                        mmu = (te_g, te_g.n)
                        wl = _wl(gen_free[1], wsg)
                        for k in range(KC):
                            op("tensor", wl if k == 0 else [],
                               lambda e, _k=k, _sl=slg, _t=tcc: e.matmul(
                                   GEN[1][:], wb(_sl, 128, _k * 128),
                                   abf(O_XN + _k * 1024 + _t * 512, 512),
                                   start=(_k == 0), stop=(_k == KC - 1)),
                               inc=(te_g, 1) if k == KC - 1 else None)
                            wl = []
                        mmg = (te_g, te_g.n)
                        op("scalar", _wl(mmu, hsfree[0]),
                           lambda e: e.activation(
                               abf(O_HS, 512), GEN[0][:], AF.Silu), inc=(ae_g, 1))
                        gen_free[0] = (ae_g, ae_g.n)
                        siluc = cover("scalar", ae_cov)
                        if is_sh or no_cmul:
                            op("vector", [mmg],
                               lambda e: e.tensor_copy(
                                   abf(O_HG, 512), GEN[1][:]), inc=(ve_g, 1))
                        else:
                            op("vector", [mmg],
                               lambda e, _t=tcc: e.tensor_tensor(
                                   abf(O_HG, 512), GEN[1][:],
                                   abf(O_CR + _t * 512, 512), ALU.mult),
                               inc=(ve_g, 1))
                        gen_free[1] = (ve_g, ve_g.n)
                        he_dst = O_HE + jc * 1024 + tcc * 512
                        wlh = [siluc]
                        if pe == 0 and jc == 6 and tcc == 0:
                            wlh.append(router_tr)
                        op("vector", wlh,
                           lambda e, _d=he_dst: e.tensor_tensor(
                               abf(_d, 512), abf(O_HS, 512), abf(O_HG, 512),
                               ALU.mult), inc=(ve_g, 1))
                        hsfree[0] = (ve_g, ve_g.n)
                    wslot_free[slu] = (te_g, te_g.n)
                    wslot_free[slg] = (te_g, te_g.n)
                he_cov = cover("vector")
                if dbg and pe == (pe_list[0] if pe_list else 0) and g == 0:
                    op("gpsimd", [he_cov], lambda e: e.dma_start(
                        out=dbg_he[:], in_=abf(O_HE, 8192)), inc=(st_o, 16))
                for dc in range(KC):
                    sld, wsd = load_w(dsrc[dc], 1024)
                    for tcc in range(2):
                        gi = (dc * 2 + tcc) % 2
                        wl = _wl(gen_free[gi], wsd,
                                 he_cov if (dc == 0 and tcc == 0) else None)
                        for j in range(8):
                            op("tensor", wl if j == 0 else [],
                               lambda e, _j=j, _sl=sld, _gi=gi, _t=tcc: e.matmul(
                                   GEN[_gi][:], wb(_sl, 128, _j * 128),
                                   abf(O_HE + _j * 1024 + _t * 512, 512),
                                   start=(_j == 0), stop=(_j == 7)),
                               inc=(te_g, 1) if j == 7 else None)
                            wl = []
                        mmy = (te_g, te_g.n)
                        if pe < NPE - 1:
                            op("vector", [mmy],
                               lambda e, _gi=gi, _dc=dc, _t=tcc: e.tensor_tensor(
                                   xp_ap(_dc, _t * 512, 512),
                                   xp_ap(_dc, _t * 512, 512), GEN[_gi][:], ALU.add),
                               inc=(ve_g, 1))
                            gen_free[gi] = (ve_g, ve_g.n)
                        else:
                            sg_i = (dc * 2 + tcc) % 2
                            so = F_STG + sg_i * 512
                            op("vector", _wl(mmy, stg_free[sg_i]),
                               lambda e, _gi=gi, _dc=dc, _t=tcc, _so=so:
                               e.tensor_tensor(aff(_so, 512), GEN[_gi][:],
                                               xp_ap(_dc, _t * 512, 512), ALU.add),
                               inc=(ve_g, 1))
                            gen_free[gi] = (ve_g, ve_g.n)
                            vcv = cover("vector")
                            op("sync", [vcv],
                               lambda e, _g=g, _dc=dc, _t=tcc, _so=so: e.dma_start(
                                   out=out_t[_g, _dc, :, _t * 512:(_t + 1) * 512],
                                   in_=aff(_so, 512)), inc=(st_o, 16))
                            stg_free[sg_i] = (st_o, st_o.n)
                    wslot_free[sld] = (te_g, te_g.n)
            if n_pe < NPE or pe_list is not None:
                xpc = cover("vector")
                for dc in range(KC):
                    for tcc in range(2):
                        op("gpsimd", _wl(xpc if dc == 0 and tcc == 0 else None),
                           lambda e, _g=g, _dc=dc, _t=tcc: e.dma_start(
                               out=out_t[_g, _dc, :, _t * 512:(_t + 1) * 512],
                               in_=xp_ap(_dc, _t * 512, 512)), inc=(st_o, 16))
            moe_state["xn_free"] = (te_g, te_g.n)
            moe_state["he_free"] = (te_g, te_g.n)
            moe_state["xp_free"] = (ve_g, ve_g.n) if (n_pe == NPE and pe_list is None) else (st_o, st_o.n)
            scr_free[0] = (te_g, te_g.n)
            scr_free[1] = (te_g, te_g.n)
            ptm_free[0] = (te_bc, te_bc.n)
            ptm_free[1] = (te_bc, te_bc.n)

        op("sync", [], lambda e: e.dma_start(
            out=abf(O_IDB, 128).bitcast(mybir.dt.uint16), in_=ident_b.ap()),
           inc=(ld_p, 16))
        op("sync", [], lambda e: e.dma_start(
            out=aff(F_IDF, 16, p0=0, p=16), in_=ident_f.ap()), inc=(ld_p, 16))
        op("sync", [], lambda e: e.dma_start(
            out=abf(O_ONCB, 1).bitcast(mybir.dt.uint16), in_=ones_cb.ap()),
           inc=(ld_p, 16))
        op("sync", [], lambda e: e.dma_start(
            out=aff(F_ONC, 1), in_=ones_cf.ap()), inc=(ld_p, 16))
        op("sync", [], lambda e: e.dma_start(
            out=aff(F_ONR, 128, p0=0, p=1), in_=ones_rf.ap()), inc=(ld_p, 16))
        op("sync", [], lambda e: e.dma_start(
            out=abf(O_ONRB, 128, p0=0, p=1).bitcast(mybir.dt.uint16),
            in_=ones_rb2.ap()), inc=(ld_p, 16))
        op("sync", [], lambda e: e.dma_start(
            out=abf(O_OHB, NE * 128).bitcast(mybir.dt.uint16),
            in_=ohb_t.ap()), inc=(ld_p, 16))
        ohb_ld = (ld_p, ld_p.n)
        op("sync", [], lambda e: e.dma_start(
            out=aff(F_EPS, 2), in_=eps_t.ap()), inc=(ld_p, 16))
        op("sync", [], lambda e: e.dma_start(
            out=abf(O_MROT, 128).bitcast(mybir.dt.uint16), in_=mrot_t.ap()),
           inc=(ld_p, 16))
        op("sync", [], lambda e: e.dma_start(
            out=aff(F_T2A, NE, p0=0, p=1), in_=rb_t[:]), inc=(ld_p, 16))
        op("sync", [], lambda e: e.dma_start(
            out=aff(F_TMPW, 128, p0=0, p=1), in_=qnw_t[:]), inc=(ld_p, 16))
        op("gpsimd", [], lambda e: e.dma_start(
            out=abf(O_RW, KC * NE).rearrange("p (k e) -> p k e", e=NE),
            in_=rw_t[:]), inc=(ld_w, 16))
        npre = (ld_p, ld_p.n)
        op("tensor", [npre], lambda e: e.transpose(
            ps_s[:, 0:1], aff(F_TMPW, 128, p0=0, p=1),
            aff(F_IDF, 1, p0=0, p=1)), inc=(te_g, 1))
        op("vector", [(te_g, te_g.n)], lambda e: e.tensor_copy(
            aff(F_QNW, 1), ps_s[:, 0:1]), inc=(ve_g, 1))
        op("sync", [(ve_g, ve_g.n)], lambda e: e.dma_start(
            out=aff(F_TMPW, 128, p0=0, p=1), in_=knw_t[:]), inc=(ld_p, 16))
        op("tensor", [(ld_p, ld_p.n)], lambda e: e.transpose(
            ps_s[:, 128:129], aff(F_TMPW, 128, p0=0, p=1),
            aff(F_IDF, 1, p0=0, p=1)), inc=(te_g, 1))
        op("vector", [(te_g, te_g.n)], lambda e: e.tensor_copy(
            aff(F_KNW, 1), ps_s[:, 128:129]))
        op("tensor", [], lambda e: e.matmul(
            ps_o[:, 0:16], aff(F_ONR, 128, p0=0, p=1), aff(F_T2A, NE, p0=0, p=1),
            start=True, stop=True), inc=(te_bc, 1))
        op("vector", [(te_bc, te_bc.n)], lambda e: e.tensor_copy(
            aff(F_RBR, 16), ps_o[:, 0:16]), inc=(ve_g, 1))
        pss_free[0] = (ve_g, ve_g.n)
        pso_free[0] = (ve_g, ve_g.n)

        moe_state = {"xn_free": None, "he_free": None, "xp_free": None,
                     "xp_ready": None}
        prev_slot = [None]
        mask_free = [None]
        rope_free = [None]
        wo_mm_done = [None]

        for s in range(NSLAB):
            ss = s % 3
            g = (s - 2) // 2
            spos = (s - 2) % 2

            op("sync", _wl(rope_free[0]), lambda e, _s=s: e.dma_start(
                out=abf(O_RP, 1024).rearrange("p (a t) -> p a t", t=SLAB),
                in_=rope_t[_s].rearrange("a p t -> p a t")), inc=(ld_m, 16))
            rope_ld = (ld_m, ld_m.n)

            ln_cov = None
            for a in range(4):
                t_idx = 4 * s + a
                sl2 = a % 2
                xt_off = O_XT + sl2 * 2048
                wl = _wl(xtile_free[sl2],
                         moe_state["he_free"] if a < 2 else None,
                         moe_state["xn_free"] if a < 2 else None)
                op("gpsimd", wl, lambda e, _t=t_idx, _o=xt_off: e.dma_start(
                    out=abf(_o, 2048).rearrange("p (k j) -> p k j", j=128),
                    in_=x_t[_t]), inc=(ld_x, 16))
                xl = (ld_x, ld_x.n)
                srcf = (lambda _o=xt_off: (lambda k: abf(_o + k * 128, 128)))()
                dstf = (lambda _a=a: (lambda k: abf(O_XN + k * 512 + _a * 128, 128)))()
                ln_cov = emit_ln_tile(srcf, (abf(O_SQ, 2048), abf(xt_off, 2048)),
                                      O_SQ, dstf, first_waits=[xl, sq_free[0]] if sq_free[0] else [xl])
                sq_free[0] = ln_cov
                xtile_free[sl2] = ln_cov
            xn_cov = ln_cov

            sts = {}
            scr_prev = {0: [scr_free[0]], 1: [scr_free[1]]}
            k_ready = None
            for it in range(H + 3):
                if it < H:
                    hh = it
                    gi = hh % 2
                    mm = emit_proj_mms(wk_t[hh],
                                       lambda k: abf(O_XN + k * 512, 512), gi,
                                       extra=_wl(xn_cov if hh == 0 else None))
                    sts[hh] = dict(gi=gi, is_q=False, mm=mm,
                                   s0=O_SCR + (hh % 2) * 1024,
                                   s1=O_SCR + (hh % 2) * 1024 + 512,
                                   dst=O_KC + hh * 1536 + ss * 512)
                if it - 3 >= 0:
                    rms_D(sts[it - 3])
                    scr_prev[(it - 3) % 2] = sts[it - 3]["scr_done"]
                    k_ready = sts[it - 3]["cov"]
                if it - 2 >= 0 and it - 2 < H:
                    rms_C(sts[it - 2])
                if it - 1 >= 0 and it - 1 < H:
                    hh = it - 1
                    stt_ = sts[hh]
                    rms_B(stt_, first_waits=_wl(stt_["mm"],
                                                rope_ld if hh == 0 else None,
                                                *scr_prev[hh % 2]))
                    scr_prev[hh % 2] = [None]

            scr_free[0] = sts[H - 2]["scr_done"][0]
            scr_free[1] = sts[H - 1]["scr_done"][0]

            for hh in range(H):
                gi = hh % 2
                mm = emit_proj_mms(wv_t[hh], lambda k: abf(O_XN + k * 512, 512),
                                   gi, extra=())
                vs_o = O_SCR + (hh % 2) * 1024
                op("scalar", _wl(mm, k_ready if hh == 0 else None,
                                 scr_free[hh % 2]),
                   lambda e, _o=vs_o, _gi=gi: e.activation(
                       abf(_o, 512), GEN[_gi][:], AF.Identity), inc=(ae_g, 1))
                gen_free[gi] = (ae_g, ae_g.n)
                vcov = cover("scalar", ae_cov)
                last_tr = None
                for a in range(4):
                    pa = a % 2
                    wl = _wl(vcov if a == 0 else None, pt_free[pa])
                    op("tensor", wl, lambda e, _o=vs_o, _a=a, _pa=pa: e.transpose(
                        PT[_pa][:, 0:128], abf(_o + _a * 128, 128), abf(O_IDB, 128)),
                       inc=(te_tr, 1))
                    tt = (te_tr, te_tr.n)
                    dvt = O_VT + ss * 8192 + a * 2048 + hh * 128
                    op("vector", [tt], lambda e, _d=dvt, _pa=pa: e.tensor_copy(
                        abf(_d, 128), PT[_pa][:, 0:128]), inc=(ve_pt, 1))
                    pt_free[pa] = (ve_pt, ve_pt.n)
                scr_free[hh % 2] = (te_tr, te_tr.n)
            v_ready = cover("vector")

            if dbg and s == 2:
                op("gpsimd", [xn_cov], lambda e: e.dma_start(
                    out=dbg_xn[:], in_=abf(O_XN, 8192)), inc=(st_o, 16))
                op("gpsimd", [k_ready], lambda e: e.dma_start(
                    out=dbg_kc[:], in_=abf(O_KC, 24576)), inc=(st_o, 16))
                op("gpsimd", [v_ready], lambda e: e.dma_start(
                    out=dbg_vt[:], in_=abf(O_VT, 24576)), inc=(st_o, 16))
            if s < 2:
                rope_free[0] = (ve_g, ve_g.n)
                continue

            sts = {}
            scr_prev = {0: [scr_free[0]], 1: [scr_free[1]]}
            q_cov = None
            for it in range(H + 3):
                if it < H:
                    hh = it
                    gi = hh % 2
                    mm = emit_proj_mms(wq_t[hh],
                                       lambda k: abf(O_XN + k * 512, 512), gi,
                                       extra=_wl(
                                           xn_cov if hh == 0 else None,
                                           wo_mm_done[0] if hh == 0 else None,
                                           moe_state["xn_free"] if hh == 0 else None,
                                           (ve_pt, ve_pt.n) if hh == 0 else None))
                    sts[hh] = dict(gi=gi, is_q=True, mm=mm,
                                   s0=O_SCR + (hh % 2) * 1024,
                                   s1=O_SCR + (hh % 2) * 1024 + 512,
                                   dst=O_QO + hh * 512)
                if it - 3 >= 0:
                    rms_D(sts[it - 3])
                    scr_prev[(it - 3) % 2] = sts[it - 3]["scr_done"]
                    q_cov = sts[it - 3]["cov"]
                if it - 2 >= 0 and it - 2 < H:
                    rms_C(sts[it - 2])
                if it - 1 >= 0 and it - 1 < H:
                    hh = it - 1
                    stt_ = sts[hh]
                    rms_B(stt_, first_waits=_wl(stt_["mm"], *scr_prev[hh % 2]))
                    scr_prev[hh % 2] = [None]

            scr_free[0] = sts[H - 2]["scr_done"][0]
            scr_free[1] = sts[H - 1]["scr_done"][0]
            rope_free[0] = (ve_g, ve_g.n)

            if dbg and s == 2:
                op("gpsimd", [q_cov], lambda e: e.dma_start(
                    out=dbg_q[:], in_=abf(O_QO, 8192)), inc=(st_o, 16))
            for qb in range(4):
                qbg = 4 * (s - 2) + qb
                op("sync", _wl(mask_free[0], q_cov if qb == 0 else None),
                   lambda e, _q=qbg: e.dma_start(
                       out=abf(O_MSK, 1152), in_=mask_t[_q]), inc=(ld_m, 16))
                mld = (ld_m, ld_m.n)
                bset = sorted({0, 512 - qb * 128, 1024 - qb * 128, 512, 1024,
                               1152})
                segs = []
                for c0, c1 in zip(bset[:-1], bset[1:]):
                    if c0 >= c1:
                        continue
                    if c0 < 512 - qb * 128:
                        ssl, t0 = (s - 2) % 3, qb * 128 + c0
                    elif c0 < 1024 - qb * 128:
                        ssl, t0 = (s - 1) % 3, c0 - (512 - qb * 128)
                    else:
                        ssl, t0 = s % 3, c0 - (1024 - qb * 128)
                    segs.append((c0, c1 - c0, ssl, t0))
                for hh in range(H):
                    u = qbg * H + hh
                    par = u % 2
                    wl = _wl(pss_free[0],
                             mld if hh == 0 else None,
                             q_cov if hh == 0 and qb == 0 else None,
                             k_ready if hh == 0 and qb == 0 else None,
                             v_ready if hh == 0 and qb == 0 else None)
                    qsl = abf(O_QO + hh * 512 + qb * 128, 128)
                    for (c0, n, ssl, t0) in segs:
                        ksrc = abf(O_KC + hh * 1536 + ssl * 512 + t0, n)
                        op("tensor", wl,
                           lambda e, _c=c0, _k=ksrc, _st=(c0 % 512 == 0), _q=qsl:
                           e.matmul(ps_s[:, _c:_c + _k.shape[-1]], _q, _k,
                                    start=_st, stop=False))
                        wl = []
                    for si, (c0, n, ssl, t0) in enumerate(segs):
                        op("tensor", [],
                           lambda e, _c=c0, _n=n, _l=(si == len(segs) - 1):
                           e.matmul(ps_s[:, _c:_c + _n], abf(O_IDB, 128),
                                    abf(O_MSK + _c, _n), start=False, stop=_l),
                           inc=(te_s, 1) if si == len(segs) - 1 else None)
                    if hh == H - 1:
                        mask_free[0] = (te_s, te_s.n)
                    pv_prev = None
                    pend_o = None
                    if prev_slot[0] is not None:
                        pv_prev = emit_tr_pv(prev_slot[0])
                        pend_o = prev_slot[0]
                    tsv = (te_s, te_s.n)
                    op("vector", [tsv], lambda e, _p=par: e.tensor_reduce(
                        aff(F_MNG + _p, 1), ps_s[:, 0:1152], AX.X, ALU.max,
                        negate=True))
                    vmx = cover("vector")
                    op("scalar", _wl(vmx, ptm_free[par]),
                       lambda e, _p=par: e.activation(
                           abf(O_PTM + _p * 1152, 1152), ps_s[:, 0:1152], AF.Exp,
                           bias=aff(F_MNG + _p, 1), accum_out=aff(F_Z + _p, 1)),
                       inc=(ae_x, 1))
                    pss_free[0] = (ae_x, ae_x.n)
                    aev = cover("scalar", ae_cov)
                    op("vector", [aev], lambda e, _p=par: e.reciprocal(
                        aff(F_ZR + _p, 1), aff(F_Z + _p, 1)))
                    zrc = cover("vector")
                    op("vector", [zrc], lambda e, _p=par: e.tensor_scalar_mul(
                        abf(O_PTM + _p * 1152, 1152), abf(O_PTM + _p * 1152, 1152),
                        aff(F_ZR + _p, 1)))
                    pn = cover("vector")
                    if pend_o is not None:
                        emit_o_evac(pend_o)
                    prev_slot[0] = dict(u=u, s=s, qb=qb, hh=hh, par=par,
                                        pnorm=pn, pv_prev=pv_prev)
            last_pv = emit_tr_pv(prev_slot[0])
            emit_o_evac(prev_slot[0])
            prev_slot[0] = None
            o_done = cover("scalar", ae_cov)
            if dbg and s == 2:
                op("vector", [o_done], lambda e: e.tensor_copy(
                    abf(O_SQ, 1152).bitcast(F32) if False else abf(O_SQ, 2048),
                    abf(O_SQ, 2048)))
                op("vector", [], lambda e: e.tensor_copy(
                    aff(F_LG, 512), ps_s[:, 0:512]))
                op("vector", [], lambda e: e.tensor_copy(
                    aff(F_LG + 512, 512, p0=0, p=128) if False else aff(F_ST5, 512),
                    ps_s[:, 512:1024]), inc=(ve_g, 1))
                sdump = (ve_g, ve_g.n)
                op("gpsimd", [sdump], lambda e: e.dma_start(
                    out=dbg_s[:, 0:512], in_=aff(F_LG, 512)), inc=(st_o, 16))
                op("gpsimd", [], lambda e: e.dma_start(
                    out=dbg_s[:, 512:1024], in_=aff(F_ST5, 512)), inc=(st_o, 16))
                op("gpsimd", [o_done], lambda e: e.dma_start(
                    out=dbg_o[:], in_=abf(O_QO, 8192)), inc=(st_o, 16))
                op("gpsimd", [o_done], lambda e: e.dma_start(
                    out=dbg_ptm[:], in_=abf(O_PTM, 2304)), inc=(st_o, 16))
                op("gpsimd", [o_done], lambda e: e.dma_start(
                    out=dbg_pts[:], in_=abf(O_PTS, 1152)), inc=(st_o, 16))
                op("gpsimd", [o_done], lambda e: e.dma_start(
                    out=dbg_st[:], in_=aff(F_MNG, 8).bitcast(mybir.dt.float32) if False else aff(F_MNG, 8)), inc=(st_o, 16))
                op("gpsimd", [o_done], lambda e: e.dma_start(
                    out=dbg_msk[:], in_=abf(O_MSK, 1152)), inc=(st_o, 16))

            for dc in range(KC):
                gi = dc % 2
                mm = emit_proj_mms(
                    wo_t[dc], lambda k: abf(O_QO + k * 512, 512), gi,
                    extra=_wl(o_done if dc == 0 else None,
                              moe_state["xp_free"] if dc == 0 and spos == 0 else None))
                xp_off = O_XP + dc * 1024 + spos * 512
                op("scalar", [mm], lambda e, _o=xp_off, _gi=gi: e.activation(
                    abf(_o, 512), GEN[_gi][:], AF.Identity), inc=(ae_g, 1))
                gen_free[gi] = (ae_g, ae_g.n)
            wo_mm_done[0] = (te_g, te_g.n)
            wo_evac = cover("scalar", ae_cov)
            for a in range(4):
                t_idx = 4 * s + a
                sl2 = a % 2
                xt_off = O_XT + sl2 * 2048
                wl = _wl(xtile_free[sl2], wo_evac if a == 0 else None)
                op("gpsimd", wl, lambda e, _t=t_idx, _o=xt_off: e.dma_start(
                    out=abf(_o, 2048).rearrange("p (k j) -> p k j", j=128),
                    in_=x_t[_t]), inc=(ld_x, 16))
                xl = (ld_x, ld_x.n)
                w2 = [xl] + _wl(wo_evac if a == 0 else None)
                for k in range(KC):
                    xp_off = O_XP + k * 1024 + spos * 512 + a * 128
                    op("vector", w2 if k == 0 else [],
                       lambda e, _xp=xp_off, _xo=xt_off, _k=k: e.tensor_tensor(
                           abf(_xp, 128), abf(_xp, 128), abf(_xo + _k * 128, 128),
                           ALU.add), inc=(ve_g, 1) if k == KC - 1 else None)
                    w2 = []
                xtile_free[sl2] = (ve_g, ve_g.n)
            moe_state["xp_ready"] = cover("vector")

            if spos == 1:
                if stage >= 3:
                    emit_moe(g, moe_state)
                else:
                    xpc = moe_state["xp_ready"]
                    for dc in range(KC):
                        for tcc in range(2):
                            op("gpsimd",
                               _wl(xpc if dc == 0 and tcc == 0 else None),
                               lambda e, _g=g, _dc=dc, _t=tcc: e.dma_start(
                                   out=out_t[_g, _dc, :, _t * 512:(_t + 1) * 512],
                                   in_=xp_ap(_dc, _t * 512, 512)), inc=(st_o, 16))
                    moe_state["xn_free"] = None
                    moe_state["he_free"] = None
                    moe_state["xp_free"] = (st_o, st_o.n)

        wait_only("sync", [(st_o, st_o.n)])

        @block.sync
        def _(e):
            for fn in ops["sync"]:
                fn(e)

        @block.gpsimd
        def _(e):
            for fn in ops["gpsimd"]:
                fn(e)

        @block.tensor
        def _(e):
            for fn in ops["tensor"]:
                fn(e)

        @block.vector
        def _(e):
            for fn in ops["vector"]:
                fn(e)

        @block.scalar
        def _(e):
            for fn in ops["scalar"]:
                fn(e)

    return nc



def kernel(**inputs):
    consts = _host_consts(inputs)
    in_maps = [_host_core_inputs(c, inputs, consts) for c in range(8)]
    nc = build_program(stage=3)
    res = run_bass_kernel_spmd(nc, in_maps, core_ids=list(range(8)))
    out = np.empty((B, S, DM), dtype=np.float32)
    for c in range(8):
        b = c // 4
        q0 = (c % 4) * QCHUNK
        ot = np.asarray(res.results[c]["out_t"])
        fm = ot.transpose(1, 2, 0, 3).reshape(DM, QCHUNK)
        out[b, q0:q0 + QCHUNK, :] = fm.T
    return out
